# version 01
# baseline (speedup 1.0000x reference)
import numpy as np
import concourse.bass as bass
import concourse.mybir as mybir
from concourse.bass_utils import run_bass_kernel_spmd
from concourse.bass_core_rust import IndirectOffsetOnAxis

F32 = mybir.dt.float32
BF16 = mybir.dt.bfloat16
U32 = mybir.dt.uint32
I8 = mybir.dt.int8
AF = mybir.ActivationFunctionType
ALU = mybir.AluOpType
AX = mybir.AxisListType

C = 1024
G = 8
NCORE = 8
B_, T_ = 4, 4096
TOK = B_ * T_
NT = TOK // NCORE
NTT, TT = 4, 512
NCH = 16
GCH = 128
E = 8
LCAP = 448
CAPACITY = 2560
IR = 4096
IS = 8192
LN_EPS = 1e-5
RMS_EPS = float(np.finfo(np.float32).eps)
TRASH = E * LCAP
XG_ROWS = E * LCAP + 128

ENGINES = ("pe", "dve", "act", "pool", "sp")


class Prog:
    def __init__(self):
        self.items = []

    def op(self, eng, fn, reads=(), writes=(), inc=1):
        assert eng in ENGINES
        self.items.append((eng, fn, tuple(reads), tuple(writes), inc))

    def bar(self):
        self.items.append(("BARRIER", None, (), (), 0))

    def emit(self, cur, engobj, sems):
        ctr = dict.fromkeys(ENGINES, 0)
        waited = dict.fromkeys(ENGINES, 0)
        last_w = {}
        readers = {}

        def do_wait(e2, v):
            if v > waited[e2]:
                engobj.wait_ge(sems[e2], v)
                waited[e2] = v

        for eng, fn, reads, writes, inc in self.items:
            if eng == "BARRIER":
                for e2 in ENGINES:
                    if e2 != cur and ctr[e2] > 0:
                        do_wait(e2, ctr[e2])
                continue
            need = {}
            for k in reads:
                w = last_w.get(k)
                if w is not None and need.get(w[0], 0) < w[1]:
                    need[w[0]] = w[1]
            for k in writes:
                w = last_w.get(k)
                if w is not None and need.get(w[0], 0) < w[1]:
                    need[w[0]] = w[1]
                for e2, v in readers.get(k, {}).items():
                    if need.get(e2, 0) < v:
                        need[e2] = v
            if eng == cur:
                for e2 in sorted(need):
                    if e2 == cur and cur == "pe":
                        continue
                    if e2 == cur:
                        assert need[e2] <= ctr[cur], "self-dep on future"
                    do_wait(e2, need[e2])
                inst = fn(engobj)
                if inc:
                    inst.then_inc(sems[cur], inc)
            ctr[eng] += inc
            for k in writes:
                last_w[k] = (eng, ctr[eng])
                readers[k] = {}
            for k in reads:
                readers.setdefault(k, {})[eng] = ctr[eng]


def gview(ap):
    return ap.rearrange("(g p) t -> p g t", p=128)


def build_nc(debug=False):
    nc = bass.Bass("TRN2", target_bir_lowering=False)

    dT = lambda name, shape: nc.dram_tensor(name, shape, F32, kind="ExternalInput").ap()
    xT = dT("xT", [C, NT])
    wqT = dT("wqT", [C, C]); wkT = dT("wkT", [C, C]); wvT = dT("wvT", [C, C])
    wgT = dT("wgT", [C, C]); woT = dT("woT", [C, C]); w0D = dT("w0D", [C, C])
    aD = dT("aD", [128, 64]); bD = dT("bD", [8, C])
    lngD = dT("lngD", [128, G]); lnbD = dT("lnbD", [128, G])
    lsgD = dT("lsgD", [128, G]); llrD = dT("llrD", [128, G])
    rwtD = dT("rwtD", [128, 64])
    rgD = dT("rgD", [E, C, IR]); ruD = dT("ruD", [E, C, IR]); rdD = dT("rdD", [E, IR, C])
    sgD = dT("sgD", [C, IS]); suD = dT("suD", [C, IS]); sdD = dT("sdD", [IS, C])
    identD = dT("identD", [128, 128])
    lstrD = dT("lstrD", [128, 128])
    lmoD = dT("lmoD", [128, NCH])
    lstrgD = dT("lstrgD", [128, 128])
    iotaeD = dT("iotaeD", [128, E])
    e448D = dT("e448D", [128, E])
    trashD = dT("trashD", [128, 1])
    q1hD = dT("q1hD", [128, NCH])
    psD = dT("psD", [128, NT])

    outD = nc.dram_tensor("outD", [NT, C], F32, kind="ExternalOutput").ap()
    dI = lambda name, shape, dt: nc.dram_tensor(name, shape, dt, kind="Internal").ap()
    qgDm = dI("qgDm", [C, NT], BF16)
    ygDm = dI("ygDm", [C, NT], BF16)
    grDm = dI("grDm", [C, NCH * 129], BF16)
    x1tmD = dI("x1tmD", [NT, C], BF16)
    xgD = dI("xgD", [XG_ROWS, C], BF16)
    eoutD = dI("eoutD", [E * LCAP, C], BF16)
    gsD = dI("gsD", [NCH, C], F32)
    gsAllD = dI("gsAllD", [GCH, C], F32)
    cntD = dI("cntD", [NCH, E], F32)
    cntAllD = dI("cntAllD", [GCH, E], F32)
    dbg = {}
    if debug:
        dbg["x1fm"] = nc.dram_tensor("dbg_x1fm", [128, G, NT], F32, kind="ExternalOutput").ap()
        dbg["logits"] = nc.dram_tensor("dbg_logits", [8, NT], F32, kind="ExternalOutput").ap()
        dbg["gidx"] = nc.dram_tensor("dbg_gidx", [128, 2 * NCH], U32, kind="ExternalOutput").ap()
        dbg["weff"] = nc.dram_tensor("dbg_weff", [128, 2 * NCH], F32, kind="ExternalOutput").ap()
        dbg["r2pre"] = nc.dram_tensor("dbg_r2pre", [NT, C], F32, kind="ExternalOutput").ap()
        dbg["route"] = nc.dram_tensor("dbg_route", [128, 512], F32, kind="ExternalOutput").ap()
        dbg["r2mid"] = nc.dram_tensor("dbg_r2mid", [NT, C], F32, kind="ExternalOutput").ap()
        dbg["prod4"] = nc.dram_tensor("dbg_prod4", [128, 8 * NT], F32, kind="ExternalOutput").ap()

    sb = lambda name, shape, dt: nc.alloc_sbuf_tensor(name, shape, dt)
    ident_f = sb("ident_f", [128, 128], F32)
    lstr_b = sb("lstr_b", [128, 128], BF16)
    lmo_b = sb("lmo_b", [128, NCH], BF16)
    lstrg_b = sb("lstrg_b", [128, 128], BF16)
    onesf_b = sb("onesf_b", [128, 128], BF16)
    ones1_b = sb("ones1_b", [1, 128], BF16)
    ones1_f = sb("ones1_f", [1, 128], F32)
    invC_b = sb("invC_b", [128, 1], BF16)
    invC_f = sb("invC_f", [128, 1], F32)
    onec_b = sb("onec_b", [128, 1], BF16)
    iotae = sb("iotae", [128, E], F32)
    e448 = sb("e448", [128, E], F32)
    trashp = sb("trashp", [128, 1], F32)
    q1h = sb("q1h", [128, NCH], F32)
    psrep = sb("psrep", [128, NT], BF16)
    lng = sb("lng", [128, G], F32); lnb = sb("lnb", [128, G], F32)
    lsg = sb("lsg", [128, G], F32); lr = sb("lr", [128, G], F32)
    rwt_f = sb("rwt_f", [128, 64], F32)
    a_b = sb("a_b", [128, 64], BF16)
    b_b = sb("b_b", [8, C], BF16)
    rows_f = sb("rows_f", [1, 2048], F32)
    rows_b = sb("rows_b", [1, 1024], BF16)
    kab_b = sb("kab_b", [8, TT], BF16)
    gsums = sb("gsums", [128, G * NCH], F32)
    gsT = sb("gsT", [NCH, C], F32)
    gsall_b = sb("gsall_b", [128, C], BF16)
    carr = sb("carr", [128, G * NCH], F32)
    logit_sb = sb("logit_sb", [8, NT], F32)
    e0f = sb("e0f", [128, NCH], F32); e1f = sb("e1f", [128, NCH], F32)
    rw0 = sb("rw0", [128, NCH], F32); rw1 = sb("rw1", [128, NCH], F32)
    oh0_b = sb("oh0_b", [128, NCH * E], BF16)
    oh1_b = sb("oh1_b", [128, NCH * E], BF16)
    ohs_b = sb("ohs_b", [128, NCH * E], BF16)
    ohk0_b = sb("ohk0_b", [128, NCH * E], BF16)
    ohk1_b = sb("ohk1_b", [128, NCH * E], BF16)
    ohks_b = sb("ohks_b", [128, NCH * E], BF16)
    cnt_sb = sb("cnt_sb", [8, NCH], F32)
    cntT_f = sb("cntT_f", [NCH, E], F32)
    cg_b = sb("cg_b", [128, E], BF16)
    pfx_sb = sb("pfx_sb", [128, E], F32)
    kcnt_sb = sb("kcnt_sb", [8, NCH], F32)
    kcntT_b = sb("kcntT_b", [NCH, E], BF16)
    lpfx_sb = sb("lpfx_sb", [NCH, E], F32)
    gidx0_u = sb("gidx0_u", [128, NCH], U32); gidx1_u = sb("gidx1_u", [128, NCH], U32)
    w0e = sb("w0e", [128, NCH], F32); w1e = sb("w1e", [128, NCH], F32)
    x1fm = sb("x1fm", [128, G * NT], BF16)
    ltm = sb("ltm", [128, 8], F32)
    topv = sb("topv", [128, 8], F32)
    topi = sb("topi", [128, 8], U32)
    s8a = sb("s8a", [128, 8], F32)
    s8b = sb("s8b", [128, 8], F32)
    s8c = sb("s8c", [128, 8], F32)
    mk_b = sb("mk_b", [128, 8], BF16)
    mk2_b = sb("mk2_b", [128, 8], BF16)
    mkl_b = sb("mkl_b", [NCH, 8], BF16)
    mkl2_b = sb("mkl2_b", [NCH, 8], BF16)
    epsr1 = sb("epsr1", [1, 1], F32)
    epsr2 = sb("epsr2", [1, 1], F32)
    eps128 = sb("eps128", [128, 1], F32)
    c1a = sb("c1a", [128, 1], F32)
    c1b = sb("c1b", [128, 1], F32)
    c1c = sb("c1c", [128, 1], F32)
    idxu = sb("idxu", [128, 2], U32)

    KB = 1024
    _raw_base = nc._sbuf_addr_for_side("left")
    sc_base = ((_raw_base + 63) // 64) * 64
    SCRATCH = 136 * KB
    nc.alloc_sbuf_tensor("scratch_reserve", [128, SCRATCH + (sc_base - _raw_base)], I8)

    def sat(name, shape, dt, at):
        t = nc.alloc_sbuf_tensor_at(name, shape, dt, offset=sc_base + at)
        return t

    xsl = [sat("xsl0", [128, G, TT], F32, 0), sat("xsl1", [128, G, TT], F32, 16 * KB)]
    wsl = [sat("wsl0", [128, G, 1024], BF16, 32 * KB), sat("wsl1", [128, G, 1024], BF16, 48 * KB)]
    h_b = sat("h_b", [128, G, TT], BF16, 64 * KB)
    xb_b = sat("xb_b", [128, G, TT], BF16, 72 * KB)
    sq_b = sat("sq_b", [128, G, TT], BF16, 80 * KB)
    K_b = sq_b
    V_b = sat("V_b", [128, G, TT], BF16, 88 * KB)
    gate_b = sat("gate_b", [128, G, TT], BF16, 96 * KB)
    qgO = sat("qgO", [128, G, TT], BF16, 104 * KB)
    ygO = sat("ygO", [128, G, TT], BF16, 112 * KB)
    gstg = sat("gstg", [128, G, 4, 129], BF16, 120 * KB)
    fA = sat("fA", [128, 512], F32, 129 * KB)
    qgB = sat("qgB", [128, G, TT], BF16, 48 * KB)
    ygB = sat("ygB", [128, G, TT], BF16, 56 * KB)
    gzB = sat("gzB", [128, G, 4, 129], BF16, 64 * KB)
    gshB = sat("gshB", [128, G, TT], BF16, 73 * KB)
    og_b = sat("og_b", [128, G, TT], BF16, 81 * KB)
    ftmp = sat("ftmp", [128, G, TT], F32, 89 * KB)
    ftmp2 = sat("ftmp2", [128, G, TT], F32, 105 * KB)
    x1stg = sat("x1stg", [128, C], BF16, 121 * KB)
    x1tmB = sat("x1tmB", [128, C], BF16, 0)
    r2acc = sat("r2acc", [128, NCH * C], F32, 2 * KB)
    xgt = sat("xgt", [128, G * LCAP], BF16, 66 * KB)
    prodb = sat("prodb", [128, 32 * LCAP], BF16, 73 * KB)
    f3wA = [sat("f3wA0", [128, 4096], BF16, 101 * KB), sat("f3wA1", [128, 4096], BF16, 109 * KB)]
    f3wB = [sat("f3wB0", [128, 4096], BF16, 117 * KB), sat("f3wB1", [128, 4096], BF16, 125 * KB)]
    eoutb = sat("eoutb", [128, C], BF16, 133 * KB)
    gsil = sat("gsil", [128, LCAP], BF16, 135 * KB)
    gt0 = sat("gt0", [128, C], BF16, 66 * KB)
    gt1 = sat("gt1", [128, C], BF16, 68 * KB)
    prod4 = sat("prod4", [128, 8 * NT], BF16, 66 * KB)
    f4sg = [sat("f4sg0", [128, G, 256], BF16, 98 * KB), sat("f4sg1", [128, G, 256], BF16, 102 * KB)]
    f4su = [sat("f4su0", [128, G, 256], BF16, 106 * KB), sat("f4su1", [128, G, 256], BF16, 110 * KB)]
    f4sd = sat("f4sd", [128, 8, C], BF16, 114 * KB)
    gsil4 = sat("gsil4", [128, TT], BF16, 130 * KB)
    upb4 = sat("upb4", [128, TT], BF16, 131 * KB)
    f5tmp = sat("f5tmp", [128, C], F32, 132 * KB)

    pb = [nc.alloc_psum_tensor(f"pb{i}", [128, 512], F32) for i in range(8)]
    PK = [f"pb{i}" for i in range(8)]

    P = Prog()
    op, bar = P.op, P.bar

    def dma(q, out_ap, in_ap, reads, writes, **kw):
        op(q, lambda e, o=out_ap, i=in_ap, kw=kw: e.dma_start(out=o, in_=i, **kw),
           reads=reads, writes=writes, inc=16)

    def mmgrp(psum_ap, pairs, reads, bank_key):
        def fn(e, ps=psum_ap, prs=tuple(pairs)):
            n = len(prs)
            inst = None
            for i, (l, r) in enumerate(prs):
                inst = e.matmul(ps, l, r, start=(i == 0), stop=(i == n - 1))
            return inst
        op("pe", fn, reads=reads, writes=(bank_key,), inc=1)

    def dve(fn, reads, writes):
        op("dve", fn, reads=reads, writes=writes, inc=1)

    def act(fn, reads, writes):
        op("act", fn, reads=reads, writes=writes, inc=1)

    dma("pool", ident_f[:], identD, (), ("ident_f",))
    dma("pool", lstr_b[:], lstrD, (), ("lstr_b",))
    dma("pool", lmo_b[:], lmoD, (), ("lmo_b",))
    dma("pool", lstrg_b[:], lstrgD, (), ("lstrg_b",))
    dma("pool", iotae[:], iotaeD, (), ("iotae",))
    dma("pool", e448[:], e448D, (), ("e448",))
    dma("pool", trashp[:], trashD, (), ("trashp",))
    dma("pool", q1h[:], q1hD, (), ("q1h",))
    dma("pool", psrep[:], psD, (), ("psrep",))
    dma("pool", lng[:], lngD, (), ("lng",))
    dma("pool", lnb[:], lnbD, (), ("lnb",))
    dma("pool", lsg[:], lsgD, (), ("lsg",))
    dma("pool", lr[:], llrD, (), ("lr",))
    dma("pool", rwt_f[:], rwtD, (), ("rwt",))
    dma("pool", a_b[:], aD, (), ("a_b",))
    dma("pool", b_b[:], bD, (), ("b_b",))
    act(lambda e: e.activation(lr[:], lr[:], AF.Exp), ("lr",), ("lr",))
    dve(lambda e: e.tensor_scalar_min(lr[:], lr[:], 1.0), ("lr",), ("lr",))
    dve(lambda e: e.memset(onesf_b[:], 1.0), (), ("onesf_b",))
    dve(lambda e: e.memset(ones1_b[:], 1.0), (), ("ones1_b",))
    dve(lambda e: e.memset(ones1_f[:], 1.0), (), ("ones1_f",))
    dve(lambda e: e.memset(invC_b[:], 1.0 / 1024.0), (), ("invC_b",))
    dve(lambda e: e.memset(invC_f[:], 1.0 / 1024.0), (), ("invC_f",))
    dve(lambda e: e.memset(onec_b[:], 1.0), (), ("onec_b",))
    dve(lambda e: e.memset(epsr1[:], LN_EPS), (), ("epsr1",))
    dve(lambda e: e.memset(epsr2[:], RMS_EPS), (), ("epsr2",))
    dve(lambda e: e.memset(eps128[:], LN_EPS), (), ("eps128",))
    dve(lambda e: e.memset(gstg[:], 0.0), (), ("gstg",))
    zsrc = gstg[:].rearrange("p g k o -> p (g k o)")
    for r0 in range(0, XG_ROWS, 128):
        n = min(128, XG_ROWS - r0)
        dma("sp", xgD[r0:r0 + n, :], zsrc[0:n, 0:C], ("gstg",), (f"xgZ{r0}",))
    bar()

    AW = [("wg", wgT), ("wq", wqT), ("wk", wkT), ("wv", wvT), ("w0", w0D)]
    widx = [0]

    def load_w(apD):
        s = widx[0] % 2
        widx[0] += 1
        dma("pool", wsl[s][:], gview(apD), (), (f"wsl{s}",))
        return s

    def ln_rows(mkey_bank, skey_bank, eps_ap, eps_key, bf16_reps):
        dve(lambda e: e.tensor_copy(rows_f[0:1, 0:512], pb[mkey_bank][0:1, :]), (PK[mkey_bank],), ("r_m",))
        dve(lambda e: e.tensor_mul(rows_f[0:1, 1536:2048], rows_f[0:1, 0:512], rows_f[0:1, 0:512]),
            ("r_m",), ("r_t",))
        dve(lambda e: e.tensor_sub(rows_f[0:1, 512:1024], pb[skey_bank][0:1, :], rows_f[0:1, 1536:2048]),
            (PK[skey_bank], "r_t"), ("r_v",))
        act(lambda e: e.activation(rows_f[0:1, 1536:2048], rows_f[0:1, 512:1024], AF.Sqrt,
                                   bias=eps_ap[:]), ("r_v", eps_key), ("r_t",))
        dve(lambda e: e.reciprocal(rows_f[0:1, 1024:1536], rows_f[0:1, 1536:2048]), ("r_t",), ("r_rs",))
        if bf16_reps:
            dve(lambda e: e.tensor_copy(rows_b[0:1, 0:512], rows_f[0:1, 0:512]), ("r_m",), ("rb_m",))
            dve(lambda e: e.tensor_copy(rows_b[0:1, 512:1024], rows_f[0:1, 1024:1536]), ("r_rs",), ("rb_rs",))
            mmgrp(pb[5][:], [(ones1_b[:], rows_b[0:1, 0:512])], ("ones1_b", "rb_m"), PK[5])
            mmgrp(pb[6][:], [(ones1_b[:], rows_b[0:1, 512:1024])], ("ones1_b", "rb_rs"), PK[6])
        else:
            mmgrp(pb[5][:], [(ones1_f[:], rows_f[0:1, 0:512])], ("ones1_f", "r_m"), PK[5])
            mmgrp(pb[6][:], [(ones1_f[:], rows_f[0:1, 1024:1536])], ("ones1_f", "r_rs"), PK[6])

    for j in range(NTT):
        tsl = slice(j * TT, (j + 1) * TT)
        xs = j % 2
        xk = f"xsl{xs}"
        dma("sp", xsl[xs][:], gview(xT)[:, :, tsl], (), (xk,))
        dve(lambda e, xs=xs: e.tensor_copy(xb_b[:], xsl[xs][:]), (xk,), ("xb",))
        dve(lambda e: e.tensor_mul(sq_b[:], xb_b[:], xb_b[:]), ("xb",), ("sqK",))
        mmgrp(pb[4][0:1, :], [(invC_b[:], xb_b[:, g, :]) for g in range(G)], ("invC_b", "xb"), PK[4])
        mmgrp(pb[7][0:1, :], [(invC_b[:], sq_b[:, g, :]) for g in range(G)], ("invC_b", "sqK"), PK[7])
        ln_rows(4, 7, epsr1, "epsr1", bf16_reps=True)
        for g in range(G):
            dve(lambda e, g=g, xs=xs: e.tensor_sub(fA[:], xsl[xs][:, g, :], pb[5][:]),
                (xk, PK[5]), ("fA",))
            dve(lambda e: e.tensor_mul(fA[:], fA[:], pb[6][:]), ("fA", PK[6]), ("fA",))
            dve(lambda e, g=g: e.tensor_scalar(h_b[:, g, :], fA[:], lng[:, g:g + 1],
                                               lnb[:, g:g + 1], ALU.mult, ALU.add),
                ("fA", "lng", "lnb"), ("h",))
        swg = load_w(wgT)
        for d in range(G):
            rot, rk = pb[d % 3], PK[d % 3]
            mmgrp(rot[:], [(wsl[swg][:, g, d * 128:(d + 1) * 128], xb_b[:, g, :]) for g in range(G)],
                  (f"wsl{swg}", "xb"), rk)
            act(lambda e, d=d, rot=rot: e.activation(gate_b[:, d, :], rot[:], AF.Sigmoid),
                (rk,), ("gateb",))
        swq = load_w(wqT)
        for d in range(G):
            rot, rk = pb[d % 3], PK[d % 3]
            mmgrp(rot[:], [(wsl[swq][:, g, d * 128:(d + 1) * 128], h_b[:, g, :]) for g in range(G)],
                  (f"wsl{swq}", "h"), rk)
            dve(lambda e, d=d, rot=rot, j=j: e.scalar_tensor_tensor(
                qgO[:, d, :], rot[:], lr[:, d:d + 1], psrep[:, j * TT:(j + 1) * TT],
                ALU.mult, ALU.mult), (rk, "lr", "psrep"), ("qgO",))
        swk = load_w(wkT)
        for d in range(G):
            rot, rk = pb[d % 3], PK[d % 3]
            mmgrp(rot[:], [(wsl[swk][:, g, d * 128:(d + 1) * 128], h_b[:, g, :]) for g in range(G)],
                  (f"wsl{swk}", "h"), rk)
            act(lambda e, d=d, rot=rot: e.activation(K_b[:, d, :], rot[:], AF.Copy), (rk,), ("sqK",))
        swv = load_w(wvT)
        for d in range(G):
            rot, rk = pb[d % 3], PK[d % 3]
            mmgrp(rot[:], [(wsl[swv][:, g, d * 128:(d + 1) * 128], h_b[:, g, :]) for g in range(G)],
                  (f"wsl{swv}", "h"), rk)
            act(lambda e, d=d, rot=rot: e.activation(V_b[:, d, :], rot[:], AF.Copy), (rk,), ("Vb",))
        dve(lambda e: e.tensor_mul(qgO[:], qgO[:], gate_b[:]), ("qgO", "gateb"), ("qgO",))
        mmgrp(pb[3][0:8, :], [(a_b[:, g * 8:(g + 1) * 8], K_b[:, g, :]) for g in range(G)],
              ("a_b", "sqK"), PK[3])
        dve(lambda e: e.tensor_copy(kab_b[:], pb[3][0:8, :]), (PK[3],), ("kab",))
        sw0 = load_w(w0D)
        for d in range(G):
            rot, rk = pb[d % 3], PK[d % 3]
            prs = [(wsl[sw0][:, g, d * 128:(d + 1) * 128], K_b[:, g, :]) for g in range(G)]
            prs.append((b_b[0:8, d * 128:(d + 1) * 128], kab_b[:]))
            mmgrp(rot[:], prs, (f"wsl{sw0}", "sqK", "b_b", "kab"), rk)
            dve(lambda e, d=d, rot=rot: e.tensor_sub(h_b[:, d, :], rot[:], V_b[:, d, :]),
                (rk, "Vb"), ("h",))
            dve(lambda e, d=d, rot=rot: e.tensor_mul(ygO[:, d, :], rot[:], gate_b[:, d, :]),
                (rk, "gateb"), ("ygO",))
            dve(lambda e, d=d: e.tensor_mul(
                gstg[:, d, :, 1:129],
                h_b[:, d, :].rearrange("p (k o) -> p k o", o=128),
                K_b[:, d, :].rearrange("p (k o) -> p k o", o=128)),
                ("h", "sqK"), ("gstg",))
            for k in range(4):
                dve(lambda e, d=d, k=k, j=j: e.reduce_sum(
                    gsums[:, d * NCH + j * 4 + k: d * NCH + j * 4 + k + 1],
                    gstg[:, d, k, 1:129], AX.X), ("gstg",), ("gsums",))
        dma("sp", gview(qgDm)[:, :, tsl], qgO[:], ("qgO",), (f"qgD{j}",))
        dma("sp", gview(ygDm)[:, :, tsl], ygO[:], ("ygO",), (f"ygD{j}",))
        dma("sp", gview(grDm)[:, :, j * 516:(j + 1) * 516],
            gstg[:].rearrange("p g k o -> p g (k o)"), ("gstg",), (f"grD{j}",))
    bar()

    for g in range(G):
        op("pe", lambda e, g=g: e.transpose(pb[7][0:16, 0:128], gsums[:, g * NCH:(g + 1) * NCH],
                                            ident_f[:]),
           reads=("gsums", "ident_f"), writes=(PK[7],), inc=1)
        dve(lambda e, g=g: e.tensor_copy(gsT[:, g * 128:(g + 1) * 128], pb[7][0:16, 0:128]),
            (PK[7],), ("gsT",))
    dma("sp", gsD, gsT[:], ("gsT",), ("gsD",))
    op("pool", lambda e: e.collective_compute(
        "AllGather", ALU.bypass, replica_groups=[list(range(NCORE))],
        ins=[gsD], outs=[gsAllD]), reads=("gsD",), writes=("gsAllD",), inc=1)
    dma("pool", gsall_b[:], gsAllD, ("gsAllD",), ("gsall",))
    for g in range(G):
        rot, rk = pb[g % 2], PK[g % 2]
        mmgrp(rot[:, 0:16], [(gsall_b[:, g * 128:(g + 1) * 128], lmo_b[:])], ("gsall", "lmo_b"), rk)
        dve(lambda e, g=g, rot=rot: e.tensor_copy(carr[:, g * NCH:(g + 1) * NCH], rot[:, 0:16]),
            (rk,), ("carr",))
    bar()

    dma("pool", wsl[0][:], gview(woT), (), ("wsl0",))
    for j in range(NTT):
        tsl = slice(j * TT, (j + 1) * TT)
        xs = j % 2
        xk = f"xsl{xs}"
        dma("sp", xsl[xs][:], gview(xT)[:, :, tsl], (), (xk,))
        dma("sp", qgB[:], gview(qgDm)[:, :, tsl], (f"qgD{j}",), ("qgB",))
        dma("sp", ygB[:], gview(ygDm)[:, :, tsl], (f"ygD{j}",), ("ygB",))
        dma("sp", gzB[:].rearrange("p g k o -> p g (k o)"),
            gview(grDm)[:, :, j * 516:(j + 1) * 516], (f"grD{j}",), ("gzB",))
        for g in range(G):
            for k in range(4):
                dve(lambda e, g=g, k=k, j=j: e.tensor_tensor_scan(
                    gshB[:, g, k * 128:(k + 1) * 128], gzB[:, g, k, 0:128], gzB[:, g, k, 0:128],
                    carr[:, g * NCH + j * 4 + k: g * NCH + j * 4 + k + 1], ALU.add, ALU.bypass),
                    ("gzB", "carr"), ("gshB",))
        dve(lambda e: e.tensor_mul(gshB[:], qgB[:], gshB[:]), ("qgB", "gshB"), ("gshB",))
        dve(lambda e: e.tensor_sub(og_b[:], ygB[:], gshB[:]), ("ygB", "gshB"), ("ogb",))
        for d in range(G):
            rot, rk = pb[d % 3], PK[d % 3]
            mmgrp(rot[:], [(wsl[0][:, g, d * 128:(d + 1) * 128], og_b[:, g, :]) for g in range(G)],
                  ("wsl0", "ogb"), rk)
            dve(lambda e, d=d, rot=rot, xs=xs: e.scalar_tensor_tensor(
                ftmp[:, d, :], rot[:], lsg[:, d:d + 1], xsl[xs][:, d, :], ALU.mult, ALU.add),
                (rk, "lsg", xk), ("ftmp",))
        dve(lambda e: e.tensor_mul(ftmp2[:], ftmp[:], ftmp[:]), ("ftmp",), ("ftmp2",))
        mmgrp(pb[4][0:1, :], [(invC_f[:], ftmp[:, g, :]) for g in range(G)], ("invC_f", "ftmp"), PK[4])
        mmgrp(pb[7][0:1, :], [(invC_f[:], ftmp2[:, g, :]) for g in range(G)], ("invC_f", "ftmp2"), PK[7])
        ln_rows(4, 7, epsr1, "epsr1", bf16_reps=False)
        for g in range(G):
            dve(lambda e, g=g: e.tensor_sub(ftmp2[:, g, :], ftmp[:, g, :], pb[5][:]),
                ("ftmp", PK[5]), ("ftmp2",))
            dve(lambda e, g=g: e.tensor_mul(ftmp2[:, g, :], ftmp2[:, g, :], pb[6][:]),
                ("ftmp2", PK[6]), ("ftmp2",))
            dve(lambda e, g=g, j=j: e.tensor_copy(
                x1fm[:, g * NT + j * TT: g * NT + (j + 1) * TT], ftmp2[:, g, :]),
                ("ftmp2",), ("x1fm",))
        dve(lambda e: e.tensor_mul(ftmp[:], ftmp2[:], ftmp2[:]), ("ftmp2",), ("ftmp",))
        mmgrp(pb[4][0:1, :], [(invC_f[:], ftmp[:, g, :]) for g in range(G)], ("invC_f", "ftmp"), PK[4])
        act(lambda e: e.activation(rows_f[0:1, 1536:2048], pb[4][0:1, :], AF.Sqrt,
                                   bias=epsr2[:]), (PK[4], "epsr2"), ("r_t",))
        dve(lambda e: e.reciprocal(rows_f[0:1, 1024:1536], rows_f[0:1, 1536:2048]), ("r_t",), ("r_rs",))
        mmgrp(pb[5][:], [(ones1_f[:], rows_f[0:1, 1024:1536])], ("ones1_f", "r_rs"), PK[5])
        for g in range(G):
            dve(lambda e, g=g: e.tensor_mul(ftmp[:, g, :], ftmp2[:, g, :], pb[5][:]),
                ("ftmp2", PK[5]), ("ftmp",))
        mmgrp(pb[3][0:8, :], [(rwt_f[:, g * 8:(g + 1) * 8], ftmp[:, g, :]) for g in range(G)],
              ("rwt", "ftmp"), PK[3])
        dve(lambda e, tsl=tsl: e.tensor_copy(logit_sb[:, tsl], pb[3][0:8, :]), (PK[3],), ("logit",))
        for k in range(4):
            for g in range(G):
                rot = pb[7] if g % 2 == 0 else pb[2]
                rk = PK[7] if g % 2 == 0 else PK[2]
                op("pe", lambda e, g=g, k=k, rot=rot: e.transpose(
                    rot[:, 0:128], ftmp2[:, g, k * 128:(k + 1) * 128], ident_f[:]),
                    reads=("ftmp2", "ident_f"), writes=(rk,), inc=1)
                dve(lambda e, g=g, rot=rot: e.tensor_copy(x1stg[:, g * 128:(g + 1) * 128],
                                                          rot[:, 0:128]), (rk,), ("x1stg",))
            dma("sp", x1tmD[(j * 4 + k) * 128:(j * 4 + k + 1) * 128, :], x1stg[:],
                ("x1stg",), (f"x1tmD{j * 4 + k}",))
    bar()
    if debug:
        dma("pool", dbg["x1fm"], x1fm[:].rearrange("p (g t) -> p g t", g=G), ("x1fm",), ("d1",))
        dma("sp", dbg["logits"], logit_sb[:], ("logit",), ("d2",))
        bar()

    for k in range(NCH):
        op("pe", lambda e, k=k: e.transpose(pb[7][:, 0:8], logit_sb[:, k * 128:(k + 1) * 128],
                                            ident_f[0:8, 0:8]),
           reads=("logit", "ident_f"), writes=(PK[7],), inc=1)
        dve(lambda e: e.tensor_copy(ltm[:], pb[7][:, 0:8]), (PK[7],), ("ltm",))
        dve(lambda e: e.max(topv[:], ltm[:]), ("ltm",), ("topv",))
        dve(lambda e: e.max_index(topi[:], topv[:], ltm[:]), ("topv", "ltm"), ("topi",))
        dve(lambda e, k=k: e.tensor_copy(e0f[:, k:k + 1], topi[:, 0:1]), ("topi",), ("e0f",))
        dve(lambda e, k=k: e.tensor_copy(e1f[:, k:k + 1], topi[:, 1:2]), ("topi",), ("e1f",))
        dve(lambda e: e.tensor_sub(c1a[:], topv[:, 0:1], topv[:, 1:2]), ("topv",), ("c1a",))
        act(lambda e, k=k: e.activation(rw0[:, k:k + 1], c1a[:], AF.Sigmoid), ("c1a",), ("rw0",))
        dve(lambda e, k=k: e.tensor_scalar(rw1[:, k:k + 1], rw0[:, k:k + 1], -1.0, 1.0,
                                           ALU.mult, ALU.add), ("rw0",), ("rw1",))
        dve(lambda e, k=k: e.tensor_scalar(s8a[:], iotae[:], e0f[:, k:k + 1], None, ALU.is_equal),
            ("iotae", "e0f"), ("s8a",))
        dve(lambda e, k=k: e.tensor_scalar(s8b[:], iotae[:], e1f[:, k:k + 1], None, ALU.is_equal),
            ("iotae", "e1f"), ("s8b",))
        dve(lambda e, k=k: e.tensor_copy(oh0_b[:, k * 8:(k + 1) * 8], s8a[:]), ("s8a",), ("oh0",))
        dve(lambda e, k=k: e.tensor_copy(oh1_b[:, k * 8:(k + 1) * 8], s8b[:]), ("s8b",), ("oh1",))
        dve(lambda e: e.tensor_add(s8c[:], s8a[:], s8b[:]), ("s8a", "s8b"), ("s8c",))
        dve(lambda e, k=k: e.tensor_copy(ohs_b[:, k * 8:(k + 1) * 8], s8c[:]), ("s8c",), ("ohs",))
        mmgrp(pb[3][0:8, 0:1], [(ohs_b[:, k * 8:(k + 1) * 8], onec_b[:])], ("ohs", "onec_b"), PK[3])
        dve(lambda e, k=k: e.tensor_copy(cnt_sb[:, k:k + 1], pb[3][0:8, 0:1]), (PK[3],), ("cnt",))
    op("pe", lambda e: e.transpose(pb[7][0:16, 0:8], cnt_sb[:], ident_f[0:8, 0:8]),
       reads=("cnt", "ident_f"), writes=(PK[7],), inc=1)
    dve(lambda e: e.tensor_copy(cntT_f[:], pb[7][0:16, 0:8]), (PK[7],), ("cntT",))
    dma("sp", cntD, cntT_f[:], ("cntT",), ("cntD",))
    op("pool", lambda e: e.collective_compute(
        "AllGather", ALU.bypass, replica_groups=[list(range(NCORE))],
        ins=[cntD], outs=[cntAllD]), reads=("cntD",), writes=("cntAllD",), inc=1)
    dma("pool", cg_b[:], cntAllD, ("cntAllD",), ("cg",))
    mmgrp(pb[4][:, 0:8], [(lstrg_b[:], cg_b[:])], ("lstrg_b", "cg"), PK[4])
    dve(lambda e: e.tensor_copy(pfx_sb[:], pb[4][:, 0:8]), (PK[4],), ("pfx",))
    bar()

    for k in range(NCH):
        dve(lambda e, k=k: e.tensor_scalar(s8a[:], pfx_sb[:], q1h[:, k:k + 1], None, ALU.mult),
            ("pfx", "q1h"), ("s8a",))
        dve(lambda e: e.tensor_copy(mk_b[:], s8a[:]), ("s8a",), ("mkb",))
        dve(lambda e: e.tensor_sub(s8c[:], s8a[:], mk_b[:]), ("s8a", "mkb"), ("s8c",))
        dve(lambda e: e.tensor_copy(mk2_b[:], s8c[:]), ("s8c",), ("mk2b",))
        rot, rk = pb[k % 2], PK[k % 2]
        def grank_fn(e, k=k, rot=rot):
            e.matmul(rot[:, 0:8], lstr_b[:], ohs_b[:, k * 8:(k + 1) * 8], start=True, stop=False)
            e.matmul(rot[:, 0:8], onesf_b[:], mk_b[:], start=False, stop=False)
            return e.matmul(rot[:, 0:8], onesf_b[:], mk2_b[:], start=False, stop=True)
        op("pe", grank_fn, reads=("lstr_b", "ohs", "onesf_b", "mkb", "mk2b"), writes=(rk,), inc=1)
        dve(lambda e, rot=rot: e.tensor_scalar(s8b[:], rot[:, 0:8], float(CAPACITY), None, ALU.is_lt),
            (rk,), ("s8b",))
        dve(lambda e, k=k: e.tensor_mul(s8a[:], s8b[:], oh0_b[:, k * 8:(k + 1) * 8]),
            ("s8b", "oh0"), ("s8a",))
        dve(lambda e, k=k: e.tensor_copy(ohk0_b[:, k * 8:(k + 1) * 8], s8a[:]), ("s8a",), ("ohk0",))
        dve(lambda e, k=k: e.tensor_mul(s8c[:], s8b[:], oh1_b[:, k * 8:(k + 1) * 8]),
            ("s8b", "oh1"), ("s8c",))
        dve(lambda e, k=k: e.tensor_copy(ohk1_b[:, k * 8:(k + 1) * 8], s8c[:]), ("s8c",), ("ohk1",))
        dve(lambda e: e.tensor_add(s8b[:], s8a[:], s8c[:]), ("s8a", "s8c"), ("s8b",))
        dve(lambda e, k=k: e.tensor_copy(ohks_b[:, k * 8:(k + 1) * 8], s8b[:]), ("s8b",), ("ohks",))
        mmgrp(pb[3][0:8, 0:1], [(ohks_b[:, k * 8:(k + 1) * 8], onec_b[:])], ("ohks", "onec_b"), PK[3])
        dve(lambda e, k=k: e.tensor_copy(kcnt_sb[:, k:k + 1], pb[3][0:8, 0:1]), (PK[3],), ("kcnt",))
    op("pe", lambda e: e.transpose(pb[7][0:16, 0:8], kcnt_sb[:], ident_f[0:8, 0:8]),
       reads=("kcnt", "ident_f"), writes=(PK[7],), inc=1)
    dve(lambda e: e.tensor_copy(kcntT_b[:], pb[7][0:16, 0:8]), (PK[7],), ("kcntTb",))
    mmgrp(pb[4][0:16, 0:8], [(lstr_b[0:16, 0:16], kcntT_b[:])], ("lstr_b", "kcntTb"), PK[4])
    dve(lambda e: e.tensor_copy(lpfx_sb[:], pb[4][0:16, 0:8]), (PK[4],), ("lpfx",))
    bar()
    if debug:
        dma("sp", dbg["route"][:, 0:8], pfx_sb[:], ("pfx",), ("dr1",))
        dma("sp", dbg["route"][0:16, 8:16], lpfx_sb[:], ("lpfx",), ("dr2",))
        dma("sp", dbg["route"][0:8, 16:32], cnt_sb[:], ("cnt",), ("dr3",))
        dma("sp", dbg["route"][0:8, 32:48], kcnt_sb[:], ("kcnt",), ("dr4",))
        dma("pool", dbg["route"][:, 48:176], ohs_b[:], ("ohs",), ("dr5",))
        dma("pool", dbg["route"][:, 176:304], ohks_b[:], ("ohks",), ("dr6",))
        bar()

    for k in range(NCH):
        dve(lambda e, k=k: e.tensor_scalar(s8a[0:16, :], lpfx_sb[:], ident_f[0:16, k:k + 1],
                                           None, ALU.mult), ("lpfx", "ident_f"), ("s8a",))
        dve(lambda e: e.tensor_copy(mkl_b[:], s8a[0:16, :]), ("s8a",), ("mklb",))
        dve(lambda e: e.tensor_sub(s8c[0:16, :], s8a[0:16, :], mkl_b[:]), ("s8a", "mklb"), ("s8c",))
        dve(lambda e: e.tensor_copy(mkl2_b[:], s8c[0:16, :]), ("s8c",), ("mkl2b",))
        rot, rk = pb[k % 2], PK[k % 2]
        def lslot_fn(e, k=k, rot=rot):
            e.matmul(rot[:, 0:8], lstr_b[:], ohks_b[:, k * 8:(k + 1) * 8], start=True, stop=False)
            e.matmul(rot[:, 0:8], onesf_b[0:16, :], mkl_b[:], start=False, stop=False)
            return e.matmul(rot[:, 0:8], onesf_b[0:16, :], mkl2_b[:], start=False, stop=True)
        op("pe", lslot_fn, reads=("lstr_b", "ohks", "onesf_b", "mklb", "mkl2b"), writes=(rk,), inc=1)
        dve(lambda e, rot=rot: e.tensor_add(s8a[:], rot[:, 0:8], e448[:]), (rk, "e448"), ("s8a",))
        for slot, (ohk_t, ohkkey, gidx_t, gw_t, rw_t, rwkey, ikey) in enumerate([
                (ohk0_b, "ohk0", gidx0_u, w0e, rw0, "rw0", "idxu0"),
                (ohk1_b, "ohk1", gidx1_u, w1e, rw1, "rw1", "idxu1")]):
            dve(lambda e, k=k, ohk_t=ohk_t: e.tensor_mul(s8b[:], s8a[:], ohk_t[:, k * 8:(k + 1) * 8]),
                ("s8a", ohkkey), ("s8b",))
            dve(lambda e: e.reduce_sum(c1a[:], s8b[:], AX.X), ("s8b",), ("c1a",))
            dve(lambda e, k=k, ohk_t=ohk_t: e.reduce_sum(c1b[:], ohk_t[:, k * 8:(k + 1) * 8], AX.X),
                (ohkkey,), ("c1b",))
            dve(lambda e, k=k, gidx_t=gidx_t: e.tensor_copy(gidx_t[:, k:k + 1], c1a[:]),
                ("c1a",), (f"gidx{slot}",))
            dve(lambda e, k=k, gw_t=gw_t, rw_t=rw_t: e.tensor_mul(gw_t[:, k:k + 1], rw_t[:, k:k + 1],
                                                                  c1b[:]), (rwkey, "c1b"), (f"w{slot}e",))
            dve(lambda e: e.tensor_scalar(c1c[:], c1b[:], -1.0, 1.0, ALU.mult, ALU.add),
                ("c1b",), ("c1c",))
            dve(lambda e: e.tensor_scalar(c1c[:], trashp[:], c1c[:], None, ALU.mult),
                ("trashp", "c1c"), ("c1c",))
            dve(lambda e: e.tensor_add(c1a[:], c1a[:], c1c[:]), ("c1a", "c1c"), ("c1a",))
            dve(lambda e, slot=slot: e.tensor_copy(idxu[:, slot:slot + 1], c1a[:]), ("c1a",), (ikey,))
        dma("sp", x1tmB[:], x1tmD[k * 128:(k + 1) * 128, :], (f"x1tmD{k}",), ("x1tmB",))
        op("pool", lambda e: e.indirect_dma_start(
            out=xgD, out_offset=IndirectOffsetOnAxis(idxu[:, 0:1], 0),
            in_=x1tmB[:], in_offset=None),
            reads=("idxu0", "x1tmB"), writes=(f"xgS0_{k}",), inc=16)
        op("pool", lambda e: e.indirect_dma_start(
            out=xgD, out_offset=IndirectOffsetOnAxis(idxu[:, 1:2], 0),
            in_=x1tmB[:], in_offset=None),
            reads=("idxu1", "x1tmB"), writes=(f"xgS1_{k}",), inc=16)
    if debug:
        dma("sp", dbg["gidx"][:, 0:NCH], gidx0_u[:], ("gidx0",), ("d3",))
        dma("sp", dbg["gidx"][:, NCH:2 * NCH], gidx1_u[:], ("gidx1",), ("d4",))
        dma("sp", dbg["weff"][:, 0:NCH], w0e[:], ("w0e",), ("d5",))
        dma("sp", dbg["weff"][:, NCH:2 * NCH], w1e[:], ("w1e",), ("d6",))
    bar()

    SLOTS = [(0, 128), (128, 128), (256, 128), (384, 64)]
    for e_ in range(E):
        for g in range(G):
            op("sp", lambda en, e_=e_, g=g: en.dma_start(
                out=xgt[:, g * LCAP:(g + 1) * LCAP],
                in_=xgD[e_ * LCAP:(e_ + 1) * LCAP, g * 128:(g + 1) * 128], transpose=True),
                reads=(), writes=("xgt",), inc=16)
        for pk_ in range(8):
            isl = slice(pk_ * 512, (pk_ + 1) * 512)
            s = pk_ % 2
            dma("pool", f3wA[s][:].rearrange("p (g i) -> p g i", g=G),
                rgD[e_, :, isl].rearrange("(g p) i -> p g i", p=128), (), (f"f3wA{s}",))
            dma("pool", f3wB[s][:].rearrange("p (g i) -> p g i", g=G),
                ruD[e_, :, isl].rearrange("(g p) i -> p g i", p=128), (), (f"f3wB{s}",))
            for ic in range(4):
                i_ = pk_ * 4 + ic
                bg, bu = (2 * i_) % 4, (2 * i_ + 1) % 4
                mmgrp(pb[bg][:, 0:LCAP],
                      [(f3wA[s][:].rearrange("p (g i) -> p g i", g=G)[:, g, ic * 128:(ic + 1) * 128],
                        xgt[:, g * LCAP:(g + 1) * LCAP]) for g in range(G)],
                      (f"f3wA{s}", "xgt"), PK[bg])
                mmgrp(pb[bu][:, 0:LCAP],
                      [(f3wB[s][:].rearrange("p (g i) -> p g i", g=G)[:, g, ic * 128:(ic + 1) * 128],
                        xgt[:, g * LCAP:(g + 1) * LCAP]) for g in range(G)],
                      (f"f3wB{s}", "xgt"), PK[bu])
                act(lambda en, bg=bg: en.activation(gsil[:], pb[bg][:, 0:LCAP], AF.Silu),
                    (PK[bg],), ("gsil",))
                dve(lambda en, i_=i_, bu=bu: en.tensor_mul(
                    prodb[:, i_ * LCAP:(i_ + 1) * LCAP], gsil[:], pb[bu][:, 0:LCAP]),
                    ("gsil", PK[bu]), ("prodb",))
        for pk_ in range(8):
            s = pk_ % 2
            dma("pool", f3wA[s][:].rearrange("p (i c) -> p i c", i=4),
                rdD[e_, pk_ * 512:(pk_ + 1) * 512, :].rearrange("(i p) c -> p i c", p=128),
                (), (f"f3wA{s}",))
            for ic in range(4):
                i_ = pk_ * 4 + ic
                def down_fn(en, s=s, ic=ic, i_=i_):
                    inst = None
                    for si, (s0, sn) in enumerate(SLOTS):
                        for cc in range(2):
                            inst = en.matmul(
                                pb[si * 2 + cc][0:sn, :],
                                prodb[:, i_ * LCAP + s0: i_ * LCAP + s0 + sn],
                                f3wA[s][:].rearrange("p (i c) -> p i c", i=4)[:, ic, cc * 512:(cc + 1) * 512],
                                start=(i_ == 0), stop=(i_ == 31))
                    return inst
                op("pe", down_fn, reads=("prodb", f"f3wA{s}"),
                   writes=tuple(PK[i] for i in range(8)), inc=1)
        for si, (s0, sn) in enumerate(SLOTS):
            dve(lambda en, si=si, sn=sn: en.tensor_copy(eoutb[0:sn, 0:512], pb[si * 2][0:sn, :]),
                (PK[si * 2],), ("eoutb",))
            dve(lambda en, si=si, sn=sn: en.tensor_copy(eoutb[0:sn, 512:1024], pb[si * 2 + 1][0:sn, :]),
                (PK[si * 2 + 1],), ("eoutb",))
            dma("sp", eoutD[e_ * LCAP + s0: e_ * LCAP + s0 + sn, :], eoutb[0:sn, :],
                ("eoutb",), (f"eoutD{e_}_{si}",))
    bar()

    for k in range(NCH):
        dma("sp", x1tmB[:], x1tmD[k * 128:(k + 1) * 128, :], (f"x1tmD{k}",), ("x1tmB",))
        dve(lambda e, k=k: e.tensor_copy(r2acc[:, k * C:(k + 1) * C], x1tmB[:]),
            ("x1tmB",), ("r2acc",))
        op("pool", lambda e, k=k: e.indirect_dma_start(
            out=gt0[:], out_offset=None, in_=eoutD,
            in_offset=IndirectOffsetOnAxis(gidx0_u[:, k:k + 1], 0)),
            reads=("gidx0",), writes=("gt0",), inc=16)
        op("pool", lambda e, k=k: e.indirect_dma_start(
            out=gt1[:], out_offset=None, in_=eoutD,
            in_offset=IndirectOffsetOnAxis(gidx1_u[:, k:k + 1], 0)),
            reads=("gidx1",), writes=("gt1",), inc=16)
        dve(lambda e, k=k: e.scalar_tensor_tensor(
            r2acc[:, k * C:(k + 1) * C], gt0[:], w0e[:, k:k + 1], r2acc[:, k * C:(k + 1) * C],
            ALU.mult, ALU.add), ("gt0", "w0e", "r2acc"), ("r2acc",))
        dve(lambda e, k=k: e.scalar_tensor_tensor(
            r2acc[:, k * C:(k + 1) * C], gt1[:], w1e[:, k:k + 1], r2acc[:, k * C:(k + 1) * C],
            ALU.mult, ALU.add), ("gt1", "w1e", "r2acc"), ("r2acc",))
    bar()
    if debug:
        for k in range(NCH):
            dma("sp", dbg["r2mid"][k * 128:(k + 1) * 128, :], r2acc[:, k * C:(k + 1) * C],
                ("r2acc",), (f"dm_{k}",))
        bar()

    import os
    NSEC = int(os.environ.get("F4NSEC", "8"))
    F4BAR = int(os.environ.get("F4BAR", "0"))
    for sec in range(NSEC):
        if F4BAR >= 1:
            bar()
        for qq in range(4):
            isl = slice(sec * 1024 + qq * 256, sec * 1024 + (qq + 1) * 256)
            sq_ = qq % 2
            dma("pool", f4sg[sq_][:], sgD[:, isl].rearrange("(g p) i -> p g i", p=128),
                (), (f"f4sg{sq_}",))
            dma("pool", f4su[sq_][:], suD[:, isl].rearrange("(g p) i -> p g i", p=128),
                (), (f"f4su{sq_}",))
            for icq in range(2):
                ic8 = qq * 2 + icq
                dma("pool", f4sd[:, ic8, :],
                    sdD[sec * 1024 + ic8 * 128: sec * 1024 + (ic8 + 1) * 128, :], (), ("f4sd",))
                tc_list = list(range(4))
                if int(os.environ.get("F4TCREV", "0")):
                    tc_list = tc_list[::-1]
                for tc in tc_list:
                    if F4BAR >= 4:
                        bar()
                    nn = ic8 * 4 + tc
                    bg, bu = (2 * nn) % 4, (2 * nn + 1) % 4
                    mmgrp(pb[bg][:], [(f4sg[sq_][:, g, icq * 128:(icq + 1) * 128],
                                       x1fm[:, g * NT + tc * TT: g * NT + (tc + 1) * TT])
                                      for g in range(G)], (f"f4sg{sq_}", "x1fm"), PK[bg])
                    mmgrp(pb[bu][:], [(f4su[sq_][:, g, icq * 128:(icq + 1) * 128],
                                       x1fm[:, g * NT + tc * TT: g * NT + (tc + 1) * TT])
                                      for g in range(G)], (f"f4su{sq_}", "x1fm"), PK[bu])
                    F4MODE = int(os.environ.get("F4MODE", "0"))
                    if F4MODE in (1, 3):
                        dve(lambda en, bg=bg: en.tensor_copy(f5tmp[:, 0:512], pb[bg][:]),
                            (PK[bg],), ("f5tmp",))
                        act(lambda en: en.activation(gsil4[:], f5tmp[:, 0:512], AF.Silu),
                            ("f5tmp",), ("gsil4",))
                    else:
                        act(lambda en, bg=bg: en.activation(gsil4[:], pb[bg][:], AF.Silu),
                            (PK[bg],), ("gsil4",))
                    if F4MODE in (2, 3):
                        act(lambda en, bu=bu: en.activation(gsil4[:, 0:512] if False else upb4[:], pb[bu][:], AF.Copy),
                            (PK[bu],), ("upb4",))
                        dve(lambda en, ic8=ic8, tc=tc: en.tensor_mul(
                            prod4[:, ic8 * NT + tc * TT: ic8 * NT + (tc + 1) * TT], gsil4[:], upb4[:]),
                            ("gsil4", "upb4"), ("prod4",))
                    else:
                        dve(lambda en, ic8=ic8, tc=tc, bu=bu: en.tensor_mul(
                            prod4[:, ic8 * NT + tc * TT: ic8 * NT + (tc + 1) * TT], gsil4[:], pb[bu][:]),
                            ("gsil4", PK[bu]), ("prod4",))
                if F4BAR >= 3:
                    bar()
        if F4BAR >= 2:
            bar()
        dve(lambda e: e.tensor_copy(s8a[:], iotae[:]), ("prod4", "iotae"), ("dummy4",))
        tch_order = list(range(NCH))
        if int(os.environ.get("F4REV", "0")):
            tch_order = tch_order[::-1]
        for tch in tch_order:
            for cc in range(2):
                if F4BAR >= 5:
                    bar()
                bank, bk = pb[4 + (2 * tch + cc) % 4], PK[4 + (2 * tch + cc) % 4]
                mmgrp(bank[:], [(prod4[:, ic8 * NT + tch * 128: ic8 * NT + (tch + 1) * 128],
                                 f4sd[:, ic8, cc * 512:(cc + 1) * 512]) for ic8 in range(8)],
                      ("prod4", "f4sd", "dummy4"), bk)
                dve(lambda en, bank=bank: en.tensor_copy(f5tmp[:, 0:512], bank[:]),
                    (bk,), ("f5tmp",))
                dve(lambda en, tch=tch, cc=cc: en.tensor_add(
                    r2acc[:, tch * C + cc * 512: tch * C + (cc + 1) * 512], f5tmp[:, 0:512],
                    r2acc[:, tch * C + cc * 512: tch * C + (cc + 1) * 512]),
                    ("f5tmp", "r2acc"), ("r2acc",))
    bar()
    if debug:
        dma("pool", dbg["prod4"], prod4[:], ("prod4",), ("dp4",))
        for k in range(NCH):
            dma("sp", dbg["r2pre"][k * 128:(k + 1) * 128, :], r2acc[:, k * C:(k + 1) * C],
                ("r2acc",), (f"d7_{k}",))
        bar()

    for k in range(NCH):
        r2s = r2acc[:, k * C:(k + 1) * C]
        dve(lambda e, r2s=r2s: e.reduce_sum(c1a[:], r2s, AX.X), ("r2acc",), ("c1a",))
        dve(lambda e, r2s=r2s: e.tensor_mul(f5tmp[:], r2s, r2s), ("r2acc",), ("f5tmp",))
        dve(lambda e: e.reduce_sum(c1b[:], f5tmp[:], AX.X), ("f5tmp",), ("c1b",))
        dve(lambda e: e.tensor_scalar_mul(c1a[:], c1a[:], 1.0 / 1024.0), ("c1a",), ("c1a",))
        dve(lambda e: e.tensor_scalar_mul(c1b[:], c1b[:], 1.0 / 1024.0), ("c1b",), ("c1b",))
        dve(lambda e: e.tensor_mul(c1c[:], c1a[:], c1a[:]), ("c1a",), ("c1c",))
        dve(lambda e: e.tensor_sub(c1b[:], c1b[:], c1c[:]), ("c1b", "c1c"), ("c1b",))
        act(lambda e: e.activation(c1c[:], c1b[:], AF.Sqrt, bias=eps128[:]), ("c1b", "eps128"), ("c1c",))
        dve(lambda e: e.reciprocal(c1b[:], c1c[:]), ("c1c",), ("c1b",))
        dve(lambda e, r2s=r2s: e.tensor_scalar(f5tmp[:], r2s, c1a[:], c1b[:], ALU.subtract, ALU.mult),
            ("r2acc", "c1a", "c1b"), ("f5tmp",))
        dma("sp", outD[k * 128:(k + 1) * 128, :], f5tmp[:], ("f5tmp",), (f"outD{k}",))
    bar()
    return nc, P


def _build_and_emit(debug=False):
    nc, P = build_nc(debug=debug)
    import contextlib
    es = contextlib.ExitStack()
    sems = {e: es.enter_context(nc.semaphore(f"sem_{e}")) for e in ENGINES}
    block = es.enter_context(nc.Block())
    with es:
        block.tensor(lambda eng: P.emit("pe", eng, sems))
        block.vector(lambda eng: P.emit("dve", eng, sems))
        block.scalar(lambda eng: P.emit("act", eng, sems))
        block.gpsimd(lambda eng: P.emit("pool", eng, sems))
        block.sync(lambda eng: P.emit("sp", eng, sems))
    return nc



def _token_ids():
    ids = []
    for c in range(NCORE):
        rows = [np.arange((q * NCORE + c) * 128, (q * NCORE + c + 1) * 128) for q in range(NCH)]
        ids.append(np.concatenate(rows))
    return ids


def _host_prep(inputs):
    x = np.asarray(inputs["L0_attn_x"], dtype=np.float32).reshape(TOK, C)
    ids = _token_ids()
    regroup = lambda v: np.ascontiguousarray(np.asarray(v, np.float32).reshape(G, 128).T)
    regroup2 = lambda m: np.ascontiguousarray(
        np.asarray(m, np.float32).reshape(G, 128, -1).transpose(1, 0, 2).reshape(128, -1))
    ident = np.eye(128, dtype=np.float32)
    lstrict = np.triu(np.ones((128, 128), np.float32), 1)
    gg = np.arange(GCH)
    lmask = ((gg[:, None] < gg[None, :]) & (gg[:, None] // 32 == gg[None, :] // 32)).astype(np.float32)
    gofr = (gg % 16) * 8 + gg // 16
    lstrg = (gofr[:, None] < gg[None, :]).astype(np.float32)
    iotae_h = np.broadcast_to(np.arange(E, dtype=np.float32), (128, E)).copy()
    common = {
        "wqT": np.ascontiguousarray(np.asarray(inputs["L0_attn_Wq"], np.float32).T),
        "wkT": np.ascontiguousarray(np.asarray(inputs["L0_attn_Wk"], np.float32).T),
        "wvT": np.ascontiguousarray(np.asarray(inputs["L0_attn_Wv"], np.float32).T),
        "wgT": np.ascontiguousarray(np.asarray(inputs["L0_attn_Wg"], np.float32).T),
        "woT": np.ascontiguousarray(np.asarray(inputs["L0_attn_Wo"], np.float32).T),
        "w0D": np.ascontiguousarray(np.asarray(inputs["L0_attn_W0"], np.float32)),
        "aD": regroup2(inputs["L0_attn_A_init"]),
        "bD": np.ascontiguousarray(np.asarray(inputs["L0_attn_B_init"], np.float32)),
        "lngD": regroup(inputs["L0_attn_ln_g"]),
        "lnbD": regroup(inputs["L0_attn_ln_b"]),
        "lsgD": regroup(inputs["L0_attn_ls_g"]),
        "llrD": regroup(inputs["L0_attn_log_lr"]),
        "rwtD": regroup2(np.asarray(inputs["L0_ffn_router_w"], np.float32).T),
        "rgD": np.ascontiguousarray(np.asarray(inputs["L0_ffn_rg"], np.float32)),
        "ruD": np.ascontiguousarray(np.asarray(inputs["L0_ffn_ru"], np.float32)),
        "rdD": np.ascontiguousarray(np.asarray(inputs["L0_ffn_rd"], np.float32)),
        "sgD": np.ascontiguousarray(np.asarray(inputs["L0_ffn_sg"], np.float32)),
        "suD": np.ascontiguousarray(np.asarray(inputs["L0_ffn_su"], np.float32)),
        "sdD": np.ascontiguousarray(np.asarray(inputs["L0_ffn_sd"], np.float32)),
        "identD": ident,
        "lstrD": lstrict,
        "iotaeD": iotae_h,
        "e448D": iotae_h * float(LCAP),
        "trashD": (TRASH + np.arange(128, dtype=np.float32)).reshape(128, 1),
        "lstrgD": lstrg,
    }
    in_maps = []
    for c in range(NCORE):
        own_g = np.arange(NCH) * NCORE + c
        q1hot = np.zeros((128, NCH), np.float32)
        for q in range(NCH):
            q1hot[q * NCORE + c, q] = 1.0
        tok = ids[c]
        pos = (tok % T_).astype(np.float64) + 1.0
        ps = (1.0 / (1.0 + 0.1 * np.log(pos))).astype(np.float32)
        m = dict(common)
        m["xT"] = np.ascontiguousarray(x[tok].T)
        m["lmoD"] = np.ascontiguousarray(lmask[gofr][:, own_g])
        m["q1hD"] = q1hot
        m["psD"] = np.broadcast_to(ps, (128, NT)).copy()
        in_maps.append(m)
    return in_maps, ids


_NC_CACHE = {}


def kernel(**inputs):
    in_maps, ids = _host_prep(inputs)
    if "nc" not in _NC_CACHE:
        _NC_CACHE["nc"] = _build_and_emit(debug=False)
    nc = _NC_CACHE["nc"]
    res = run_bass_kernel_spmd(nc, in_maps, core_ids=list(range(NCORE)))
    out = np.empty((TOK, C), dtype=np.float32)
    for c in range(NCORE):
        out[ids[c]] = res.results[c]["outD"]
    return out.reshape(B_, T_, C)


# version 02
# speedup vs baseline: 1.0065x; 1.0065x over previous
import numpy as np
import concourse.bass as bass
import concourse.mybir as mybir
from concourse.bass_utils import run_bass_kernel_spmd
from concourse.bass_core_rust import IndirectOffsetOnAxis

F32 = mybir.dt.float32
BF16 = mybir.dt.bfloat16
U32 = mybir.dt.uint32
I8 = mybir.dt.int8
AF = mybir.ActivationFunctionType
ALU = mybir.AluOpType
AX = mybir.AxisListType

C = 1024
G = 8
NCORE = 8
B_, T_ = 4, 4096
TOK = B_ * T_
NT = TOK // NCORE
NTT, TT = 4, 512
NCH = 16
GCH = 128
E = 8
LCAP = 448
CAPACITY = 2560
IR = 4096
IS = 8192
LN_EPS = 1e-5
RMS_EPS = float(np.finfo(np.float32).eps)
TRASH = E * LCAP
XG_ROWS = E * LCAP + 128

ENGINES = ("pe", "dve", "act", "pool", "sp")


class Prog:
    def __init__(self):
        self.items = []

    def op(self, eng, fn, reads=(), writes=(), inc=1):
        assert eng in ENGINES
        self.items.append((eng, fn, tuple(reads), tuple(writes), inc))

    def bar(self):
        self.items.append(("BARRIER", None, (), (), 0))

    def emit(self, cur, engobj, sems):
        ctr = dict.fromkeys(ENGINES, 0)
        waited = dict.fromkeys(ENGINES, 0)
        last_w = {}
        readers = {}

        def do_wait(e2, v):
            if v > waited[e2]:
                engobj.wait_ge(sems[e2], v)
                waited[e2] = v

        for eng, fn, reads, writes, inc in self.items:
            if eng == "BARRIER":
                for e2 in ENGINES:
                    if e2 != cur and ctr[e2] > 0:
                        do_wait(e2, ctr[e2])
                continue
            need = {}
            for k in reads:
                w = last_w.get(k)
                if w is not None and need.get(w[0], 0) < w[1]:
                    need[w[0]] = w[1]
            for k in writes:
                w = last_w.get(k)
                if w is not None and need.get(w[0], 0) < w[1]:
                    need[w[0]] = w[1]
                for e2, v in readers.get(k, {}).items():
                    if need.get(e2, 0) < v:
                        need[e2] = v
            if eng == cur:
                for e2 in sorted(need):
                    if e2 == cur and cur == "pe":
                        continue
                    if e2 == cur:
                        assert need[e2] <= ctr[cur], "self-dep on future"
                    do_wait(e2, need[e2])
                inst = fn(engobj)
                if inc:
                    inst.then_inc(sems[cur], inc)
            ctr[eng] += inc
            for k in writes:
                last_w[k] = (eng, ctr[eng])
                readers[k] = {}
            for k in reads:
                readers.setdefault(k, {})[eng] = ctr[eng]


def gview(ap):
    return ap.rearrange("(g p) t -> p g t", p=128)


def build_nc(debug=False):
    nc = bass.Bass("TRN2", target_bir_lowering=False)

    dT = lambda name, shape: nc.dram_tensor(name, shape, F32, kind="ExternalInput").ap()
    xT = dT("xT", [C, NT])
    wqT = dT("wqT", [C, C]); wkT = dT("wkT", [C, C]); wvT = dT("wvT", [C, C])
    wgT = dT("wgT", [C, C]); woT = dT("woT", [C, C]); w0D = dT("w0D", [C, C])
    aD = dT("aD", [128, 64]); bD = dT("bD", [8, C])
    lngD = dT("lngD", [128, G]); lnbD = dT("lnbD", [128, G])
    lsgD = dT("lsgD", [128, G]); llrD = dT("llrD", [128, G])
    rwtD = dT("rwtD", [128, 64])
    rgD = dT("rgD", [E, C, IR]); ruD = dT("ruD", [E, C, IR]); rdD = dT("rdD", [E, IR, C])
    sgD = dT("sgD", [C, IS]); suD = dT("suD", [C, IS]); sdD = dT("sdD", [IS, C])
    identD = dT("identD", [128, 128])
    lstrD = dT("lstrD", [128, 128])
    lmoD = dT("lmoD", [128, NCH])
    lstrgD = dT("lstrgD", [128, 128])
    iotaeD = dT("iotaeD", [128, E])
    e448D = dT("e448D", [128, E])
    trashD = dT("trashD", [128, 1])
    q1hD = dT("q1hD", [128, NCH])
    psD = dT("psD", [128, NT])

    outD = nc.dram_tensor("outD", [NT, C], F32, kind="ExternalOutput").ap()
    dI = lambda name, shape, dt: nc.dram_tensor(name, shape, dt, kind="Internal").ap()
    qgDm = dI("qgDm", [C, NT], BF16)
    ygDm = dI("ygDm", [C, NT], BF16)
    grDm = dI("grDm", [C, NCH * 129], BF16)
    x1tmD = dI("x1tmD", [NT, C], BF16)
    xgD = dI("xgD", [XG_ROWS, C], BF16)
    eoutD = dI("eoutD", [E * LCAP, C], BF16)
    gsD = dI("gsD", [NCH, C], F32)
    gsAllD = dI("gsAllD", [GCH, C], F32)
    cntD = dI("cntD", [NCH, E], F32)
    cntAllD = dI("cntAllD", [GCH, E], F32)
    dbg = {}
    if debug:
        dbg["x1fm"] = nc.dram_tensor("dbg_x1fm", [128, G, NT], F32, kind="ExternalOutput").ap()
        dbg["logits"] = nc.dram_tensor("dbg_logits", [8, NT], F32, kind="ExternalOutput").ap()
        dbg["gidx"] = nc.dram_tensor("dbg_gidx", [128, 2 * NCH], U32, kind="ExternalOutput").ap()
        dbg["weff"] = nc.dram_tensor("dbg_weff", [128, 2 * NCH], F32, kind="ExternalOutput").ap()
        dbg["r2pre"] = nc.dram_tensor("dbg_r2pre", [NT, C], F32, kind="ExternalOutput").ap()
        dbg["route"] = nc.dram_tensor("dbg_route", [128, 512], F32, kind="ExternalOutput").ap()
        dbg["r2mid"] = nc.dram_tensor("dbg_r2mid", [NT, C], F32, kind="ExternalOutput").ap()
        dbg["prod4"] = nc.dram_tensor("dbg_prod4", [128, 8 * NT], F32, kind="ExternalOutput").ap()

    sb = lambda name, shape, dt: nc.alloc_sbuf_tensor(name, shape, dt)
    ident_f = sb("ident_f", [128, 128], F32)
    lstr_b = sb("lstr_b", [128, 128], BF16)
    lmo_b = sb("lmo_b", [128, NCH], BF16)
    lstrg_b = sb("lstrg_b", [128, 128], BF16)
    onesf_b = sb("onesf_b", [128, 128], BF16)
    ones1_b = sb("ones1_b", [1, 128], BF16)
    ones1_f = sb("ones1_f", [1, 128], F32)
    invC_b = sb("invC_b", [128, 1], BF16)
    invC_f = sb("invC_f", [128, 1], F32)
    onec_b = sb("onec_b", [128, 1], BF16)
    iotae = sb("iotae", [128, E], F32)
    e448 = sb("e448", [128, E], F32)
    trashp = sb("trashp", [128, 1], F32)
    q1h = sb("q1h", [128, NCH], F32)
    psrep = sb("psrep", [128, NT], BF16)
    lng = sb("lng", [128, G], F32); lnb = sb("lnb", [128, G], F32)
    lsg = sb("lsg", [128, G], F32); lr = sb("lr", [128, G], F32)
    rwt_f = sb("rwt_f", [128, 64], F32)
    a_b = sb("a_b", [128, 64], BF16)
    b_b = sb("b_b", [8, C], BF16)
    rows_f = sb("rows_f", [1, 2048], F32)
    rows_b = sb("rows_b", [1, 1024], BF16)
    kab_b = sb("kab_b", [8, TT], BF16)
    gsums = sb("gsums", [128, G * NCH], F32)
    gsT = sb("gsT", [NCH, C], F32)
    gsall_b = sb("gsall_b", [128, C], BF16)
    carr = sb("carr", [128, G * NCH], F32)
    logit_sb = sb("logit_sb", [8, NT], F32)
    e0f = sb("e0f", [128, NCH], F32); e1f = sb("e1f", [128, NCH], F32)
    rw0 = sb("rw0", [128, NCH], F32); rw1 = sb("rw1", [128, NCH], F32)
    oh0_b = sb("oh0_b", [128, NCH * E], BF16)
    oh1_b = sb("oh1_b", [128, NCH * E], BF16)
    ohs_b = sb("ohs_b", [128, NCH * E], BF16)
    ohk0_b = sb("ohk0_b", [128, NCH * E], BF16)
    ohk1_b = sb("ohk1_b", [128, NCH * E], BF16)
    ohks_b = sb("ohks_b", [128, NCH * E], BF16)
    cnt_sb = sb("cnt_sb", [8, NCH], F32)
    cntT_f = sb("cntT_f", [NCH, E], F32)
    cg_b = sb("cg_b", [128, E], BF16)
    pfx_sb = sb("pfx_sb", [128, E], F32)
    kcnt_sb = sb("kcnt_sb", [8, NCH], F32)
    kcntT_b = sb("kcntT_b", [NCH, E], BF16)
    lpfx_sb = sb("lpfx_sb", [NCH, E], F32)
    gidx0_u = sb("gidx0_u", [128, NCH], U32); gidx1_u = sb("gidx1_u", [128, NCH], U32)
    w0e = sb("w0e", [128, NCH], F32); w1e = sb("w1e", [128, NCH], F32)
    x1fm = sb("x1fm", [128, G * NT], BF16)
    ltm = sb("ltm", [128, 8], F32)
    topv = sb("topv", [128, 8], F32)
    topi = sb("topi", [128, 8], U32)
    s8a = sb("s8a", [128, 8], F32)
    s8b = sb("s8b", [128, 8], F32)
    s8c = sb("s8c", [128, 8], F32)
    mk_b = sb("mk_b", [128, 8], BF16)
    mk2_b = sb("mk2_b", [128, 8], BF16)
    mkl_b = sb("mkl_b", [NCH, 8], BF16)
    mkl2_b = sb("mkl2_b", [NCH, 8], BF16)
    epsr1 = sb("epsr1", [1, 1], F32)
    epsr2 = sb("epsr2", [1, 1], F32)
    eps128 = sb("eps128", [128, 1], F32)
    c1a = sb("c1a", [128, 1], F32)
    c1b = sb("c1b", [128, 1], F32)
    c1c = sb("c1c", [128, 1], F32)
    idxu = sb("idxu", [128, 2], U32)

    KB = 1024
    _raw_base = nc._sbuf_addr_for_side("left")
    sc_base = ((_raw_base + 63) // 64) * 64
    SCRATCH = 136 * KB
    nc.alloc_sbuf_tensor("scratch_reserve", [128, SCRATCH + (sc_base - _raw_base)], I8)

    def sat(name, shape, dt, at):
        t = nc.alloc_sbuf_tensor_at(name, shape, dt, offset=sc_base + at)
        return t

    xsl = [sat("xsl0", [128, G, TT], F32, 0), sat("xsl1", [128, G, TT], F32, 16 * KB)]
    wsl = [sat("wsl0", [128, G, 1024], BF16, 32 * KB), sat("wsl1", [128, G, 1024], BF16, 48 * KB)]
    h_b = sat("h_b", [128, G, TT], BF16, 64 * KB)
    xb_b = sat("xb_b", [128, G, TT], BF16, 72 * KB)
    sq_b = sat("sq_b", [128, G, TT], BF16, 80 * KB)
    K_b = sq_b
    V_b = sat("V_b", [128, G, TT], BF16, 88 * KB)
    gate_b = sat("gate_b", [128, G, TT], BF16, 96 * KB)
    qgO = sat("qgO", [128, G, TT], BF16, 104 * KB)
    ygO = sat("ygO", [128, G, TT], BF16, 112 * KB)
    gstg = sat("gstg", [128, G, 4, 129], BF16, 120 * KB)
    fA = sat("fA", [128, 512], F32, 129 * KB)
    qgB = sat("qgB", [128, G, TT], BF16, 48 * KB)
    ygB = sat("ygB", [128, G, TT], BF16, 56 * KB)
    gzB = sat("gzB", [128, G, 4, 129], BF16, 64 * KB)
    gshB = sat("gshB", [128, G, TT], BF16, 73 * KB)
    og_b = sat("og_b", [128, G, TT], BF16, 81 * KB)
    ftmp = sat("ftmp", [128, G, TT], F32, 89 * KB)
    ftmp2 = sat("ftmp2", [128, G, TT], F32, 105 * KB)
    x1stg = sat("x1stg", [128, C], BF16, 121 * KB)
    x1tmB = sat("x1tmB", [128, C], BF16, 0)
    r2acc = sat("r2acc", [128, NCH * C], F32, 2 * KB)
    xgt = sat("xgt", [128, G * LCAP], BF16, 66 * KB)
    prodb = sat("prodb", [128, 32 * LCAP], BF16, 73 * KB)
    f3wA = [sat("f3wA0", [128, 4096], BF16, 101 * KB), sat("f3wA1", [128, 4096], BF16, 109 * KB)]
    f3wB = [sat("f3wB0", [128, 4096], BF16, 117 * KB), sat("f3wB1", [128, 4096], BF16, 125 * KB)]
    eoutb = sat("eoutb", [128, C], BF16, 133 * KB)
    gsil = sat("gsil", [128, LCAP], BF16, 135 * KB)
    gt0 = sat("gt0", [128, C], BF16, 66 * KB)
    gt1 = sat("gt1", [128, C], BF16, 68 * KB)
    prod4 = sat("prod4", [128, 8 * NT], BF16, 66 * KB)
    f4sg = [sat("f4sg0", [128, G, 256], BF16, 98 * KB), sat("f4sg1", [128, G, 256], BF16, 102 * KB)]
    f4su = [sat("f4su0", [128, G, 256], BF16, 106 * KB), sat("f4su1", [128, G, 256], BF16, 110 * KB)]
    f4sd = sat("f4sd", [128, 8, C], BF16, 114 * KB)
    gsil4 = sat("gsil4", [128, TT], BF16, 130 * KB)
    upb4 = sat("upb4", [128, TT], BF16, 131 * KB)
    f5tmp = sat("f5tmp", [128, C], F32, 132 * KB)

    pb = [nc.alloc_psum_tensor(f"pb{i}", [128, 512], F32) for i in range(8)]
    PK = [f"pb{i}" for i in range(8)]

    P = Prog()
    op, bar = P.op, P.bar

    def dma(q, out_ap, in_ap, reads, writes, **kw):
        op(q, lambda e, o=out_ap, i=in_ap, kw=kw: e.dma_start(out=o, in_=i, **kw),
           reads=reads, writes=writes, inc=16)

    def mmgrp(psum_ap, pairs, reads, bank_key):
        def fn(e, ps=psum_ap, prs=tuple(pairs)):
            n = len(prs)
            inst = None
            for i, (l, r) in enumerate(prs):
                inst = e.matmul(ps, l, r, start=(i == 0), stop=(i == n - 1))
            return inst
        op("pe", fn, reads=reads, writes=(bank_key,), inc=1)

    def dve(fn, reads, writes):
        op("dve", fn, reads=reads, writes=writes, inc=1)

    def act(fn, reads, writes):
        op("act", fn, reads=reads, writes=writes, inc=1)

    dma("pool", ident_f[:], identD, (), ("ident_f",))
    dma("pool", lstr_b[:], lstrD, (), ("lstr_b",))
    dma("pool", lmo_b[:], lmoD, (), ("lmo_b",))
    dma("pool", lstrg_b[:], lstrgD, (), ("lstrg_b",))
    dma("pool", iotae[:], iotaeD, (), ("iotae",))
    dma("pool", e448[:], e448D, (), ("e448",))
    dma("pool", trashp[:], trashD, (), ("trashp",))
    dma("pool", q1h[:], q1hD, (), ("q1h",))
    dma("pool", psrep[:], psD, (), ("psrep",))
    dma("pool", lng[:], lngD, (), ("lng",))
    dma("pool", lnb[:], lnbD, (), ("lnb",))
    dma("pool", lsg[:], lsgD, (), ("lsg",))
    dma("pool", lr[:], llrD, (), ("lr",))
    dma("pool", rwt_f[:], rwtD, (), ("rwt",))
    dma("pool", a_b[:], aD, (), ("a_b",))
    dma("pool", b_b[:], bD, (), ("b_b",))
    act(lambda e: e.activation(lr[:], lr[:], AF.Exp), ("lr",), ("lr",))
    dve(lambda e: e.tensor_scalar_min(lr[:], lr[:], 1.0), ("lr",), ("lr",))
    dve(lambda e: e.memset(onesf_b[:], 1.0), (), ("onesf_b",))
    dve(lambda e: e.memset(ones1_b[:], 1.0), (), ("ones1_b",))
    dve(lambda e: e.memset(ones1_f[:], 1.0), (), ("ones1_f",))
    dve(lambda e: e.memset(invC_b[:], 1.0 / 1024.0), (), ("invC_b",))
    dve(lambda e: e.memset(invC_f[:], 1.0 / 1024.0), (), ("invC_f",))
    dve(lambda e: e.memset(onec_b[:], 1.0), (), ("onec_b",))
    dve(lambda e: e.memset(epsr1[:], LN_EPS), (), ("epsr1",))
    dve(lambda e: e.memset(epsr2[:], RMS_EPS), (), ("epsr2",))
    dve(lambda e: e.memset(eps128[:], LN_EPS), (), ("eps128",))
    dve(lambda e: e.memset(gstg[:], 0.0), (), ("gstg",))
    zsrc = gstg[:].rearrange("p g k o -> p (g k o)")
    for r0 in range(0, XG_ROWS, 128):
        n = min(128, XG_ROWS - r0)
        dma("sp", xgD[r0:r0 + n, :], zsrc[0:n, 0:C], ("gstg",), (f"xgZ{r0}",))
    bar()

    AW = [("wg", wgT), ("wq", wqT), ("wk", wkT), ("wv", wvT), ("w0", w0D)]
    widx = [0]

    def load_w(apD):
        s = widx[0] % 2
        widx[0] += 1
        dma("pool", wsl[s][:], gview(apD), (), (f"wsl{s}",))
        return s

    def ln_rows(mkey_bank, skey_bank, eps_ap, eps_key, bf16_reps):
        dve(lambda e: e.tensor_copy(rows_f[0:1, 0:512], pb[mkey_bank][0:1, :]), (PK[mkey_bank],), ("r_m",))
        dve(lambda e: e.tensor_mul(rows_f[0:1, 1536:2048], rows_f[0:1, 0:512], rows_f[0:1, 0:512]),
            ("r_m",), ("r_t",))
        dve(lambda e: e.tensor_sub(rows_f[0:1, 512:1024], pb[skey_bank][0:1, :], rows_f[0:1, 1536:2048]),
            (PK[skey_bank], "r_t"), ("r_v",))
        act(lambda e: e.activation(rows_f[0:1, 1536:2048], rows_f[0:1, 512:1024], AF.Sqrt,
                                   bias=eps_ap[:]), ("r_v", eps_key), ("r_t",))
        dve(lambda e: e.reciprocal(rows_f[0:1, 1024:1536], rows_f[0:1, 1536:2048]), ("r_t",), ("r_rs",))
        if bf16_reps:
            dve(lambda e: e.tensor_copy(rows_b[0:1, 0:512], rows_f[0:1, 0:512]), ("r_m",), ("rb_m",))
            dve(lambda e: e.tensor_copy(rows_b[0:1, 512:1024], rows_f[0:1, 1024:1536]), ("r_rs",), ("rb_rs",))
            mmgrp(pb[5][:], [(ones1_b[:], rows_b[0:1, 0:512])], ("ones1_b", "rb_m"), PK[5])
            mmgrp(pb[6][:], [(ones1_b[:], rows_b[0:1, 512:1024])], ("ones1_b", "rb_rs"), PK[6])
        else:
            mmgrp(pb[5][:], [(ones1_f[:], rows_f[0:1, 0:512])], ("ones1_f", "r_m"), PK[5])
            mmgrp(pb[6][:], [(ones1_f[:], rows_f[0:1, 1024:1536])], ("ones1_f", "r_rs"), PK[6])

    for j in range(NTT):
        tsl = slice(j * TT, (j + 1) * TT)
        xs = j % 2
        xk = f"xsl{xs}"
        dma("sp", xsl[xs][:], gview(xT)[:, :, tsl], (), (xk,))
        dve(lambda e, xs=xs: e.tensor_copy(xb_b[:], xsl[xs][:]), (xk,), ("xb",))
        dve(lambda e: e.tensor_mul(sq_b[:], xb_b[:], xb_b[:]), ("xb",), ("sqK",))
        mmgrp(pb[4][0:1, :], [(invC_b[:], xb_b[:, g, :]) for g in range(G)], ("invC_b", "xb"), PK[4])
        mmgrp(pb[7][0:1, :], [(invC_b[:], sq_b[:, g, :]) for g in range(G)], ("invC_b", "sqK"), PK[7])
        ln_rows(4, 7, epsr1, "epsr1", bf16_reps=True)
        for g in range(G):
            dve(lambda e, g=g, xs=xs: e.tensor_sub(fA[:], xsl[xs][:, g, :], pb[5][:]),
                (xk, PK[5]), ("fA",))
            dve(lambda e: e.tensor_mul(fA[:], fA[:], pb[6][:]), ("fA", PK[6]), ("fA",))
            dve(lambda e, g=g: e.tensor_scalar(h_b[:, g, :], fA[:], lng[:, g:g + 1],
                                               lnb[:, g:g + 1], ALU.mult, ALU.add),
                ("fA", "lng", "lnb"), ("h",))
        swg = load_w(wgT)
        for d in range(G):
            rot, rk = pb[d % 3], PK[d % 3]
            mmgrp(rot[:], [(wsl[swg][:, g, d * 128:(d + 1) * 128], xb_b[:, g, :]) for g in range(G)],
                  (f"wsl{swg}", "xb"), rk)
            act(lambda e, d=d, rot=rot: e.activation(gate_b[:, d, :], rot[:], AF.Sigmoid),
                (rk,), ("gateb",))
        swq = load_w(wqT)
        for d in range(G):
            rot, rk = pb[d % 3], PK[d % 3]
            mmgrp(rot[:], [(wsl[swq][:, g, d * 128:(d + 1) * 128], h_b[:, g, :]) for g in range(G)],
                  (f"wsl{swq}", "h"), rk)
            dve(lambda e, d=d, rot=rot, j=j: e.scalar_tensor_tensor(
                qgO[:, d, :], rot[:], lr[:, d:d + 1], psrep[:, j * TT:(j + 1) * TT],
                ALU.mult, ALU.mult), (rk, "lr", "psrep"), ("qgO",))
        swk = load_w(wkT)
        for d in range(G):
            rot, rk = pb[d % 3], PK[d % 3]
            mmgrp(rot[:], [(wsl[swk][:, g, d * 128:(d + 1) * 128], h_b[:, g, :]) for g in range(G)],
                  (f"wsl{swk}", "h"), rk)
            act(lambda e, d=d, rot=rot: e.activation(K_b[:, d, :], rot[:], AF.Copy), (rk,), ("sqK",))
        swv = load_w(wvT)
        for d in range(G):
            rot, rk = pb[d % 3], PK[d % 3]
            mmgrp(rot[:], [(wsl[swv][:, g, d * 128:(d + 1) * 128], h_b[:, g, :]) for g in range(G)],
                  (f"wsl{swv}", "h"), rk)
            act(lambda e, d=d, rot=rot: e.activation(V_b[:, d, :], rot[:], AF.Copy), (rk,), ("Vb",))
        dve(lambda e: e.tensor_mul(qgO[:], qgO[:], gate_b[:]), ("qgO", "gateb"), ("qgO",))
        mmgrp(pb[3][0:8, :], [(a_b[:, g * 8:(g + 1) * 8], K_b[:, g, :]) for g in range(G)],
              ("a_b", "sqK"), PK[3])
        dve(lambda e: e.tensor_copy(kab_b[:], pb[3][0:8, :]), (PK[3],), ("kab",))
        sw0 = load_w(w0D)
        for d in range(G):
            rot, rk = pb[d % 3], PK[d % 3]
            prs = [(wsl[sw0][:, g, d * 128:(d + 1) * 128], K_b[:, g, :]) for g in range(G)]
            prs.append((b_b[0:8, d * 128:(d + 1) * 128], kab_b[:]))
            mmgrp(rot[:], prs, (f"wsl{sw0}", "sqK", "b_b", "kab"), rk)
            dve(lambda e, d=d, rot=rot: e.tensor_sub(h_b[:, d, :], rot[:], V_b[:, d, :]),
                (rk, "Vb"), ("h",))
            dve(lambda e, d=d, rot=rot: e.tensor_mul(ygO[:, d, :], rot[:], gate_b[:, d, :]),
                (rk, "gateb"), ("ygO",))
            dve(lambda e, d=d: e.tensor_mul(
                gstg[:, d, :, 1:129],
                h_b[:, d, :].rearrange("p (k o) -> p k o", o=128),
                K_b[:, d, :].rearrange("p (k o) -> p k o", o=128)),
                ("h", "sqK"), ("gstg",))
            for k in range(4):
                dve(lambda e, d=d, k=k, j=j: e.reduce_sum(
                    gsums[:, d * NCH + j * 4 + k: d * NCH + j * 4 + k + 1],
                    gstg[:, d, k, 1:129], AX.X), ("gstg",), ("gsums",))
        dma("sp", gview(qgDm)[:, :, tsl], qgO[:], ("qgO",), (f"qgD{j}",))
        dma("sp", gview(ygDm)[:, :, tsl], ygO[:], ("ygO",), (f"ygD{j}",))
        dma("sp", gview(grDm)[:, :, j * 516:(j + 1) * 516],
            gstg[:].rearrange("p g k o -> p g (k o)"), ("gstg",), (f"grD{j}",))
    bar()

    for g in range(G):
        op("pe", lambda e, g=g: e.transpose(pb[7][0:16, 0:128], gsums[:, g * NCH:(g + 1) * NCH],
                                            ident_f[:]),
           reads=("gsums", "ident_f"), writes=(PK[7],), inc=1)
        dve(lambda e, g=g: e.tensor_copy(gsT[:, g * 128:(g + 1) * 128], pb[7][0:16, 0:128]),
            (PK[7],), ("gsT",))
    dma("sp", gsD, gsT[:], ("gsT",), ("gsD",))
    op("pool", lambda e: e.collective_compute(
        "AllGather", ALU.bypass, replica_groups=[list(range(NCORE))],
        ins=[gsD], outs=[gsAllD]), reads=("gsD",), writes=("gsAllD",), inc=1)
    dma("sp", xsl[0][:], gview(xT)[:, :, 0:TT], (), ("xsl0",))
    dma("sp", qgB[:], gview(qgDm)[:, :, 0:TT], ("qgD0",), ("qgB",))
    dma("sp", ygB[:], gview(ygDm)[:, :, 0:TT], ("ygD0",), ("ygB",))
    dma("sp", gzB[:].rearrange("p g k o -> p g (k o)"),
        gview(grDm)[:, :, 0:516], ("grD0",), ("gzB",))
    dma("pool", gsall_b[:], gsAllD, ("gsAllD",), ("gsall",))
    dma("pool", wsl[0][:], gview(woT), (), ("wsl0",))
    for g in range(G):
        rot, rk = pb[g % 2], PK[g % 2]
        mmgrp(rot[:, 0:16], [(gsall_b[:, g * 128:(g + 1) * 128], lmo_b[:])], ("gsall", "lmo_b"), rk)
        dve(lambda e, g=g, rot=rot: e.tensor_copy(carr[:, g * NCH:(g + 1) * NCH], rot[:, 0:16]),
            (rk,), ("carr",))
    bar()

    for j in range(NTT):
        tsl = slice(j * TT, (j + 1) * TT)
        xs = j % 2
        xk = f"xsl{xs}"
        if j > 0:
            dma("sp", xsl[xs][:], gview(xT)[:, :, tsl], (), (xk,))
            dma("sp", qgB[:], gview(qgDm)[:, :, tsl], (f"qgD{j}",), ("qgB",))
            dma("sp", ygB[:], gview(ygDm)[:, :, tsl], (f"ygD{j}",), ("ygB",))
            dma("sp", gzB[:].rearrange("p g k o -> p g (k o)"),
                gview(grDm)[:, :, j * 516:(j + 1) * 516], (f"grD{j}",), ("gzB",))
        for g in range(G):
            for k in range(4):
                dve(lambda e, g=g, k=k, j=j: e.tensor_tensor_scan(
                    gshB[:, g, k * 128:(k + 1) * 128], gzB[:, g, k, 0:128], gzB[:, g, k, 0:128],
                    carr[:, g * NCH + j * 4 + k: g * NCH + j * 4 + k + 1], ALU.add, ALU.bypass),
                    ("gzB", "carr"), ("gshB",))
        dve(lambda e: e.tensor_mul(gshB[:], qgB[:], gshB[:]), ("qgB", "gshB"), ("gshB",))
        dve(lambda e: e.tensor_sub(og_b[:], ygB[:], gshB[:]), ("ygB", "gshB"), ("ogb",))
        for d in range(G):
            rot, rk = pb[d % 3], PK[d % 3]
            mmgrp(rot[:], [(wsl[0][:, g, d * 128:(d + 1) * 128], og_b[:, g, :]) for g in range(G)],
                  ("wsl0", "ogb"), rk)
            dve(lambda e, d=d, rot=rot, xs=xs: e.scalar_tensor_tensor(
                ftmp[:, d, :], rot[:], lsg[:, d:d + 1], xsl[xs][:, d, :], ALU.mult, ALU.add),
                (rk, "lsg", xk), ("ftmp",))
        dve(lambda e: e.tensor_mul(ftmp2[:], ftmp[:], ftmp[:]), ("ftmp",), ("ftmp2",))
        mmgrp(pb[4][0:1, :], [(invC_f[:], ftmp[:, g, :]) for g in range(G)], ("invC_f", "ftmp"), PK[4])
        mmgrp(pb[7][0:1, :], [(invC_f[:], ftmp2[:, g, :]) for g in range(G)], ("invC_f", "ftmp2"), PK[7])
        ln_rows(4, 7, epsr1, "epsr1", bf16_reps=False)
        for g in range(G):
            dve(lambda e, g=g: e.tensor_sub(ftmp2[:, g, :], ftmp[:, g, :], pb[5][:]),
                ("ftmp", PK[5]), ("ftmp2",))
            dve(lambda e, g=g: e.tensor_mul(ftmp2[:, g, :], ftmp2[:, g, :], pb[6][:]),
                ("ftmp2", PK[6]), ("ftmp2",))
            dve(lambda e, g=g, j=j: e.tensor_copy(
                x1fm[:, g * NT + j * TT: g * NT + (j + 1) * TT], ftmp2[:, g, :]),
                ("ftmp2",), ("x1fm",))
        dve(lambda e: e.tensor_mul(ftmp[:], ftmp2[:], ftmp2[:]), ("ftmp2",), ("ftmp",))
        mmgrp(pb[4][0:1, :], [(invC_f[:], ftmp[:, g, :]) for g in range(G)], ("invC_f", "ftmp"), PK[4])
        act(lambda e: e.activation(rows_f[0:1, 1536:2048], pb[4][0:1, :], AF.Sqrt,
                                   bias=epsr2[:]), (PK[4], "epsr2"), ("r_t",))
        dve(lambda e: e.reciprocal(rows_f[0:1, 1024:1536], rows_f[0:1, 1536:2048]), ("r_t",), ("r_rs",))
        mmgrp(pb[5][:], [(ones1_f[:], rows_f[0:1, 1024:1536])], ("ones1_f", "r_rs"), PK[5])
        for g in range(G):
            dve(lambda e, g=g: e.tensor_mul(ftmp[:, g, :], ftmp2[:, g, :], pb[5][:]),
                ("ftmp2", PK[5]), ("ftmp",))
        mmgrp(pb[3][0:8, :], [(rwt_f[:, g * 8:(g + 1) * 8], ftmp[:, g, :]) for g in range(G)],
              ("rwt", "ftmp"), PK[3])
        dve(lambda e, tsl=tsl: e.tensor_copy(logit_sb[:, tsl], pb[3][0:8, :]), (PK[3],), ("logit",))
        for k in range(4):
            for g in range(G):
                rot = pb[7] if g % 2 == 0 else pb[2]
                rk = PK[7] if g % 2 == 0 else PK[2]
                op("pe", lambda e, g=g, k=k, rot=rot: e.transpose(
                    rot[:, 0:128], ftmp2[:, g, k * 128:(k + 1) * 128], ident_f[:]),
                    reads=("ftmp2", "ident_f"), writes=(rk,), inc=1)
                dve(lambda e, g=g, rot=rot: e.tensor_copy(x1stg[:, g * 128:(g + 1) * 128],
                                                          rot[:, 0:128]), (rk,), ("x1stg",))
            dma("sp", x1tmD[(j * 4 + k) * 128:(j * 4 + k + 1) * 128, :], x1stg[:],
                ("x1stg",), (f"x1tmD{j * 4 + k}",))
    bar()
    if debug:
        dma("pool", dbg["x1fm"], x1fm[:].rearrange("p (g t) -> p g t", g=G), ("x1fm",), ("d1",))
        dma("sp", dbg["logits"], logit_sb[:], ("logit",), ("d2",))
        bar()

    for k in range(NCH):
        op("pe", lambda e, k=k: e.transpose(pb[7][:, 0:8], logit_sb[:, k * 128:(k + 1) * 128],
                                            ident_f[0:8, 0:8]),
           reads=("logit", "ident_f"), writes=(PK[7],), inc=1)
        dve(lambda e: e.tensor_copy(ltm[:], pb[7][:, 0:8]), (PK[7],), ("ltm",))
        dve(lambda e: e.max(topv[:], ltm[:]), ("ltm",), ("topv",))
        dve(lambda e: e.max_index(topi[:], topv[:], ltm[:]), ("topv", "ltm"), ("topi",))
        dve(lambda e, k=k: e.tensor_copy(e0f[:, k:k + 1], topi[:, 0:1]), ("topi",), ("e0f",))
        dve(lambda e, k=k: e.tensor_copy(e1f[:, k:k + 1], topi[:, 1:2]), ("topi",), ("e1f",))
        dve(lambda e: e.tensor_sub(c1a[:], topv[:, 0:1], topv[:, 1:2]), ("topv",), ("c1a",))
        act(lambda e, k=k: e.activation(rw0[:, k:k + 1], c1a[:], AF.Sigmoid), ("c1a",), ("rw0",))
        dve(lambda e, k=k: e.tensor_scalar(rw1[:, k:k + 1], rw0[:, k:k + 1], -1.0, 1.0,
                                           ALU.mult, ALU.add), ("rw0",), ("rw1",))
        dve(lambda e, k=k: e.tensor_scalar(s8a[:], iotae[:], e0f[:, k:k + 1], None, ALU.is_equal),
            ("iotae", "e0f"), ("s8a",))
        dve(lambda e, k=k: e.tensor_scalar(s8b[:], iotae[:], e1f[:, k:k + 1], None, ALU.is_equal),
            ("iotae", "e1f"), ("s8b",))
        dve(lambda e, k=k: e.tensor_copy(oh0_b[:, k * 8:(k + 1) * 8], s8a[:]), ("s8a",), ("oh0",))
        dve(lambda e, k=k: e.tensor_copy(oh1_b[:, k * 8:(k + 1) * 8], s8b[:]), ("s8b",), ("oh1",))
        dve(lambda e: e.tensor_add(s8c[:], s8a[:], s8b[:]), ("s8a", "s8b"), ("s8c",))
        dve(lambda e, k=k: e.tensor_copy(ohs_b[:, k * 8:(k + 1) * 8], s8c[:]), ("s8c",), ("ohs",))
        mmgrp(pb[3][0:8, 0:1], [(ohs_b[:, k * 8:(k + 1) * 8], onec_b[:])], ("ohs", "onec_b"), PK[3])
        dve(lambda e, k=k: e.tensor_copy(cnt_sb[:, k:k + 1], pb[3][0:8, 0:1]), (PK[3],), ("cnt",))
    op("pe", lambda e: e.transpose(pb[7][0:16, 0:8], cnt_sb[:], ident_f[0:8, 0:8]),
       reads=("cnt", "ident_f"), writes=(PK[7],), inc=1)
    dve(lambda e: e.tensor_copy(cntT_f[:], pb[7][0:16, 0:8]), (PK[7],), ("cntT",))
    dma("sp", cntD, cntT_f[:], ("cntT",), ("cntD",))
    op("pool", lambda e: e.collective_compute(
        "AllGather", ALU.bypass, replica_groups=[list(range(NCORE))],
        ins=[cntD], outs=[cntAllD]), reads=("cntD",), writes=("cntAllD",), inc=1)
    dma("pool", cg_b[:], cntAllD, ("cntAllD",), ("cg",))
    mmgrp(pb[4][:, 0:8], [(lstrg_b[:], cg_b[:])], ("lstrg_b", "cg"), PK[4])
    dve(lambda e: e.tensor_copy(pfx_sb[:], pb[4][:, 0:8]), (PK[4],), ("pfx",))
    bar()

    for k in range(NCH):
        dve(lambda e, k=k: e.tensor_scalar(s8a[:], pfx_sb[:], q1h[:, k:k + 1], None, ALU.mult),
            ("pfx", "q1h"), ("s8a",))
        dve(lambda e: e.tensor_copy(mk_b[:], s8a[:]), ("s8a",), ("mkb",))
        dve(lambda e: e.tensor_sub(s8c[:], s8a[:], mk_b[:]), ("s8a", "mkb"), ("s8c",))
        dve(lambda e: e.tensor_copy(mk2_b[:], s8c[:]), ("s8c",), ("mk2b",))
        rot, rk = pb[k % 2], PK[k % 2]
        def grank_fn(e, k=k, rot=rot):
            e.matmul(rot[:, 0:8], lstr_b[:], ohs_b[:, k * 8:(k + 1) * 8], start=True, stop=False)
            e.matmul(rot[:, 0:8], onesf_b[:], mk_b[:], start=False, stop=False)
            return e.matmul(rot[:, 0:8], onesf_b[:], mk2_b[:], start=False, stop=True)
        op("pe", grank_fn, reads=("lstr_b", "ohs", "onesf_b", "mkb", "mk2b"), writes=(rk,), inc=1)
        dve(lambda e, rot=rot: e.tensor_scalar(s8b[:], rot[:, 0:8], float(CAPACITY), None, ALU.is_lt),
            (rk,), ("s8b",))
        dve(lambda e, k=k: e.tensor_mul(s8a[:], s8b[:], oh0_b[:, k * 8:(k + 1) * 8]),
            ("s8b", "oh0"), ("s8a",))
        dve(lambda e, k=k: e.tensor_copy(ohk0_b[:, k * 8:(k + 1) * 8], s8a[:]), ("s8a",), ("ohk0",))
        dve(lambda e, k=k: e.tensor_mul(s8c[:], s8b[:], oh1_b[:, k * 8:(k + 1) * 8]),
            ("s8b", "oh1"), ("s8c",))
        dve(lambda e, k=k: e.tensor_copy(ohk1_b[:, k * 8:(k + 1) * 8], s8c[:]), ("s8c",), ("ohk1",))
        dve(lambda e: e.tensor_add(s8b[:], s8a[:], s8c[:]), ("s8a", "s8c"), ("s8b",))
        dve(lambda e, k=k: e.tensor_copy(ohks_b[:, k * 8:(k + 1) * 8], s8b[:]), ("s8b",), ("ohks",))
        mmgrp(pb[3][0:8, 0:1], [(ohks_b[:, k * 8:(k + 1) * 8], onec_b[:])], ("ohks", "onec_b"), PK[3])
        dve(lambda e, k=k: e.tensor_copy(kcnt_sb[:, k:k + 1], pb[3][0:8, 0:1]), (PK[3],), ("kcnt",))
    op("pe", lambda e: e.transpose(pb[7][0:16, 0:8], kcnt_sb[:], ident_f[0:8, 0:8]),
       reads=("kcnt", "ident_f"), writes=(PK[7],), inc=1)
    dve(lambda e: e.tensor_copy(kcntT_b[:], pb[7][0:16, 0:8]), (PK[7],), ("kcntTb",))
    mmgrp(pb[4][0:16, 0:8], [(lstr_b[0:16, 0:16], kcntT_b[:])], ("lstr_b", "kcntTb"), PK[4])
    dve(lambda e: e.tensor_copy(lpfx_sb[:], pb[4][0:16, 0:8]), (PK[4],), ("lpfx",))
    bar()
    if debug:
        dma("sp", dbg["route"][:, 0:8], pfx_sb[:], ("pfx",), ("dr1",))
        dma("sp", dbg["route"][0:16, 8:16], lpfx_sb[:], ("lpfx",), ("dr2",))
        dma("sp", dbg["route"][0:8, 16:32], cnt_sb[:], ("cnt",), ("dr3",))
        dma("sp", dbg["route"][0:8, 32:48], kcnt_sb[:], ("kcnt",), ("dr4",))
        dma("pool", dbg["route"][:, 48:176], ohs_b[:], ("ohs",), ("dr5",))
        dma("pool", dbg["route"][:, 176:304], ohks_b[:], ("ohks",), ("dr6",))
        bar()

    for k in range(NCH):
        dve(lambda e, k=k: e.tensor_scalar(s8a[0:16, :], lpfx_sb[:], ident_f[0:16, k:k + 1],
                                           None, ALU.mult), ("lpfx", "ident_f"), ("s8a",))
        dve(lambda e: e.tensor_copy(mkl_b[:], s8a[0:16, :]), ("s8a",), ("mklb",))
        dve(lambda e: e.tensor_sub(s8c[0:16, :], s8a[0:16, :], mkl_b[:]), ("s8a", "mklb"), ("s8c",))
        dve(lambda e: e.tensor_copy(mkl2_b[:], s8c[0:16, :]), ("s8c",), ("mkl2b",))
        rot, rk = pb[k % 2], PK[k % 2]
        def lslot_fn(e, k=k, rot=rot):
            e.matmul(rot[:, 0:8], lstr_b[:], ohks_b[:, k * 8:(k + 1) * 8], start=True, stop=False)
            e.matmul(rot[:, 0:8], onesf_b[0:16, :], mkl_b[:], start=False, stop=False)
            return e.matmul(rot[:, 0:8], onesf_b[0:16, :], mkl2_b[:], start=False, stop=True)
        op("pe", lslot_fn, reads=("lstr_b", "ohks", "onesf_b", "mklb", "mkl2b"), writes=(rk,), inc=1)
        dve(lambda e, rot=rot: e.tensor_add(s8a[:], rot[:, 0:8], e448[:]), (rk, "e448"), ("s8a",))
        for slot, (ohk_t, ohkkey, gidx_t, gw_t, rw_t, rwkey, ikey) in enumerate([
                (ohk0_b, "ohk0", gidx0_u, w0e, rw0, "rw0", "idxu0"),
                (ohk1_b, "ohk1", gidx1_u, w1e, rw1, "rw1", "idxu1")]):
            dve(lambda e, k=k, ohk_t=ohk_t: e.tensor_mul(s8b[:], s8a[:], ohk_t[:, k * 8:(k + 1) * 8]),
                ("s8a", ohkkey), ("s8b",))
            dve(lambda e: e.reduce_sum(c1a[:], s8b[:], AX.X), ("s8b",), ("c1a",))
            dve(lambda e, k=k, ohk_t=ohk_t: e.reduce_sum(c1b[:], ohk_t[:, k * 8:(k + 1) * 8], AX.X),
                (ohkkey,), ("c1b",))
            dve(lambda e, k=k, gidx_t=gidx_t: e.tensor_copy(gidx_t[:, k:k + 1], c1a[:]),
                ("c1a",), (f"gidx{slot}",))
            dve(lambda e, k=k, gw_t=gw_t, rw_t=rw_t: e.tensor_mul(gw_t[:, k:k + 1], rw_t[:, k:k + 1],
                                                                  c1b[:]), (rwkey, "c1b"), (f"w{slot}e",))
            dve(lambda e: e.tensor_scalar(c1c[:], c1b[:], -1.0, 1.0, ALU.mult, ALU.add),
                ("c1b",), ("c1c",))
            dve(lambda e: e.tensor_scalar(c1c[:], trashp[:], c1c[:], None, ALU.mult),
                ("trashp", "c1c"), ("c1c",))
            dve(lambda e: e.tensor_add(c1a[:], c1a[:], c1c[:]), ("c1a", "c1c"), ("c1a",))
            dve(lambda e, slot=slot: e.tensor_copy(idxu[:, slot:slot + 1], c1a[:]), ("c1a",), (ikey,))
        dma("sp", x1tmB[:], x1tmD[k * 128:(k + 1) * 128, :], (f"x1tmD{k}",), ("x1tmB",))
        op("pool", lambda e: e.indirect_dma_start(
            out=xgD, out_offset=IndirectOffsetOnAxis(idxu[:, 0:1], 0),
            in_=x1tmB[:], in_offset=None),
            reads=("idxu0", "x1tmB"), writes=(f"xgS0_{k}",), inc=16)
        op("pool", lambda e: e.indirect_dma_start(
            out=xgD, out_offset=IndirectOffsetOnAxis(idxu[:, 1:2], 0),
            in_=x1tmB[:], in_offset=None),
            reads=("idxu1", "x1tmB"), writes=(f"xgS1_{k}",), inc=16)
    if debug:
        dma("sp", dbg["gidx"][:, 0:NCH], gidx0_u[:], ("gidx0",), ("d3",))
        dma("sp", dbg["gidx"][:, NCH:2 * NCH], gidx1_u[:], ("gidx1",), ("d4",))
        dma("sp", dbg["weff"][:, 0:NCH], w0e[:], ("w0e",), ("d5",))
        dma("sp", dbg["weff"][:, NCH:2 * NCH], w1e[:], ("w1e",), ("d6",))
    bar()

    SLOTS = [(0, 128), (128, 128), (256, 128), (384, 64)]
    for e_ in range(E):
        for g in range(G):
            op("sp", lambda en, e_=e_, g=g: en.dma_start(
                out=xgt[:, g * LCAP:(g + 1) * LCAP],
                in_=xgD[e_ * LCAP:(e_ + 1) * LCAP, g * 128:(g + 1) * 128], transpose=True),
                reads=(), writes=("xgt",), inc=16)
        for pk_ in range(8):
            isl = slice(pk_ * 512, (pk_ + 1) * 512)
            s = pk_ % 2
            dma("pool", f3wA[s][:].rearrange("p (g i) -> p g i", g=G),
                rgD[e_, :, isl].rearrange("(g p) i -> p g i", p=128), (), (f"f3wA{s}",))
            dma("pool", f3wB[s][:].rearrange("p (g i) -> p g i", g=G),
                ruD[e_, :, isl].rearrange("(g p) i -> p g i", p=128), (), (f"f3wB{s}",))
            for ic in range(4):
                i_ = pk_ * 4 + ic
                bg, bu = (2 * i_) % 4, (2 * i_ + 1) % 4
                mmgrp(pb[bg][:, 0:LCAP],
                      [(f3wA[s][:].rearrange("p (g i) -> p g i", g=G)[:, g, ic * 128:(ic + 1) * 128],
                        xgt[:, g * LCAP:(g + 1) * LCAP]) for g in range(G)],
                      (f"f3wA{s}", "xgt"), PK[bg])
                mmgrp(pb[bu][:, 0:LCAP],
                      [(f3wB[s][:].rearrange("p (g i) -> p g i", g=G)[:, g, ic * 128:(ic + 1) * 128],
                        xgt[:, g * LCAP:(g + 1) * LCAP]) for g in range(G)],
                      (f"f3wB{s}", "xgt"), PK[bu])
                act(lambda en, bg=bg: en.activation(gsil[:], pb[bg][:, 0:LCAP], AF.Silu),
                    (PK[bg],), ("gsil",))
                dve(lambda en, i_=i_, bu=bu: en.tensor_mul(
                    prodb[:, i_ * LCAP:(i_ + 1) * LCAP], gsil[:], pb[bu][:, 0:LCAP]),
                    ("gsil", PK[bu]), ("prodb",))
        for pk_ in range(8):
            s = pk_ % 2
            dma("pool", f3wA[s][:].rearrange("p (i c) -> p i c", i=4),
                rdD[e_, pk_ * 512:(pk_ + 1) * 512, :].rearrange("(i p) c -> p i c", p=128),
                (), (f"f3wA{s}",))
            for ic in range(4):
                i_ = pk_ * 4 + ic
                def down_fn(en, s=s, ic=ic, i_=i_):
                    inst = None
                    for si, (s0, sn) in enumerate(SLOTS):
                        for cc in range(2):
                            inst = en.matmul(
                                pb[si * 2 + cc][0:sn, :],
                                prodb[:, i_ * LCAP + s0: i_ * LCAP + s0 + sn],
                                f3wA[s][:].rearrange("p (i c) -> p i c", i=4)[:, ic, cc * 512:(cc + 1) * 512],
                                start=(i_ == 0), stop=(i_ == 31))
                    return inst
                op("pe", down_fn, reads=("prodb", f"f3wA{s}"),
                   writes=tuple(PK[i] for i in range(8)), inc=1)
        for si, (s0, sn) in enumerate(SLOTS):
            dve(lambda en, si=si, sn=sn: en.tensor_copy(eoutb[0:sn, 0:512], pb[si * 2][0:sn, :]),
                (PK[si * 2],), ("eoutb",))
            dve(lambda en, si=si, sn=sn: en.tensor_copy(eoutb[0:sn, 512:1024], pb[si * 2 + 1][0:sn, :]),
                (PK[si * 2 + 1],), ("eoutb",))
            dma("sp", eoutD[e_ * LCAP + s0: e_ * LCAP + s0 + sn, :], eoutb[0:sn, :],
                ("eoutb",), (f"eoutD{e_}_{si}",))
    bar()

    for k in range(NCH):
        dma("sp", x1tmB[:], x1tmD[k * 128:(k + 1) * 128, :], (f"x1tmD{k}",), ("x1tmB",))
        dve(lambda e, k=k: e.tensor_copy(r2acc[:, k * C:(k + 1) * C], x1tmB[:]),
            ("x1tmB",), ("r2acc",))
        op("pool", lambda e, k=k: e.indirect_dma_start(
            out=gt0[:], out_offset=None, in_=eoutD,
            in_offset=IndirectOffsetOnAxis(gidx0_u[:, k:k + 1], 0)),
            reads=("gidx0",), writes=("gt0",), inc=16)
        op("pool", lambda e, k=k: e.indirect_dma_start(
            out=gt1[:], out_offset=None, in_=eoutD,
            in_offset=IndirectOffsetOnAxis(gidx1_u[:, k:k + 1], 0)),
            reads=("gidx1",), writes=("gt1",), inc=16)
        dve(lambda e, k=k: e.scalar_tensor_tensor(
            r2acc[:, k * C:(k + 1) * C], gt0[:], w0e[:, k:k + 1], r2acc[:, k * C:(k + 1) * C],
            ALU.mult, ALU.add), ("gt0", "w0e", "r2acc"), ("r2acc",))
        dve(lambda e, k=k: e.scalar_tensor_tensor(
            r2acc[:, k * C:(k + 1) * C], gt1[:], w1e[:, k:k + 1], r2acc[:, k * C:(k + 1) * C],
            ALU.mult, ALU.add), ("gt1", "w1e", "r2acc"), ("r2acc",))
    bar()
    if debug:
        for k in range(NCH):
            dma("sp", dbg["r2mid"][k * 128:(k + 1) * 128, :], r2acc[:, k * C:(k + 1) * C],
                ("r2acc",), (f"dm_{k}",))
        bar()

    NSEC = 8
    for sec in range(NSEC):
        for qq in range(4):
            isl = slice(sec * 1024 + qq * 256, sec * 1024 + (qq + 1) * 256)
            sq_ = qq % 2
            dma("pool", f4sg[sq_][:], sgD[:, isl].rearrange("(g p) i -> p g i", p=128),
                (), (f"f4sg{sq_}",))
            dma("pool", f4su[sq_][:], suD[:, isl].rearrange("(g p) i -> p g i", p=128),
                (), (f"f4su{sq_}",))
            for icq in range(2):
                ic8 = qq * 2 + icq
                dma("pool", f4sd[:, ic8, :],
                    sdD[sec * 1024 + ic8 * 128: sec * 1024 + (ic8 + 1) * 128, :], (), ("f4sd",))
                for tc in range(4):
                    nn = ic8 * 4 + tc
                    bg, bu = (2 * nn) % 4, (2 * nn + 1) % 4
                    mmgrp(pb[bg][:], [(f4sg[sq_][:, g, icq * 128:(icq + 1) * 128],
                                       x1fm[:, g * NT + tc * TT: g * NT + (tc + 1) * TT])
                                      for g in range(G)], (f"f4sg{sq_}", "x1fm"), PK[bg])
                    mmgrp(pb[bu][:], [(f4su[sq_][:, g, icq * 128:(icq + 1) * 128],
                                       x1fm[:, g * NT + tc * TT: g * NT + (tc + 1) * TT])
                                      for g in range(G)], (f"f4su{sq_}", "x1fm"), PK[bu])
                    act(lambda en, bg=bg: en.activation(gsil4[:], pb[bg][:], AF.Silu),
                        (PK[bg],), ("gsil4",))
                    dve(lambda en, ic8=ic8, tc=tc, bu=bu: en.tensor_mul(
                        prod4[:, ic8 * NT + tc * TT: ic8 * NT + (tc + 1) * TT], gsil4[:], pb[bu][:]),
                        ("gsil4", PK[bu]), ("prod4",))
        dve(lambda e: e.tensor_copy(s8a[:], iotae[:]), ("prod4", "iotae"), ("dummy4",))
        for tch in range(NCH):
            for cc in range(2):
                bank, bk = pb[4 + (2 * tch + cc) % 4], PK[4 + (2 * tch + cc) % 4]
                mmgrp(bank[:], [(prod4[:, ic8 * NT + tch * 128: ic8 * NT + (tch + 1) * 128],
                                 f4sd[:, ic8, cc * 512:(cc + 1) * 512]) for ic8 in range(8)],
                      ("prod4", "f4sd", "dummy4"), bk)
                dve(lambda en, bank=bank: en.tensor_copy(f5tmp[:, 0:512], bank[:]),
                    (bk,), ("f5tmp",))
                dve(lambda en, tch=tch, cc=cc: en.tensor_add(
                    r2acc[:, tch * C + cc * 512: tch * C + (cc + 1) * 512], f5tmp[:, 0:512],
                    r2acc[:, tch * C + cc * 512: tch * C + (cc + 1) * 512]),
                    ("f5tmp", "r2acc"), ("r2acc",))
    bar()
    if debug:
        dma("pool", dbg["prod4"], prod4[:], ("prod4",), ("dp4",))
        for k in range(NCH):
            dma("sp", dbg["r2pre"][k * 128:(k + 1) * 128, :], r2acc[:, k * C:(k + 1) * C],
                ("r2acc",), (f"d7_{k}",))
        bar()

    for k in range(NCH):
        r2s = r2acc[:, k * C:(k + 1) * C]
        dve(lambda e, r2s=r2s: e.reduce_sum(c1a[:], r2s, AX.X), ("r2acc",), ("c1a",))
        dve(lambda e, r2s=r2s: e.tensor_mul(f5tmp[:], r2s, r2s), ("r2acc",), ("f5tmp",))
        dve(lambda e: e.reduce_sum(c1b[:], f5tmp[:], AX.X), ("f5tmp",), ("c1b",))
        dve(lambda e: e.tensor_scalar_mul(c1a[:], c1a[:], 1.0 / 1024.0), ("c1a",), ("c1a",))
        dve(lambda e: e.tensor_scalar_mul(c1b[:], c1b[:], 1.0 / 1024.0), ("c1b",), ("c1b",))
        dve(lambda e: e.tensor_mul(c1c[:], c1a[:], c1a[:]), ("c1a",), ("c1c",))
        dve(lambda e: e.tensor_sub(c1b[:], c1b[:], c1c[:]), ("c1b", "c1c"), ("c1b",))
        act(lambda e: e.activation(c1c[:], c1b[:], AF.Sqrt, bias=eps128[:]), ("c1b", "eps128"), ("c1c",))
        dve(lambda e: e.reciprocal(c1b[:], c1c[:]), ("c1c",), ("c1b",))
        dve(lambda e, r2s=r2s: e.tensor_scalar(f5tmp[:], r2s, c1a[:], c1b[:], ALU.subtract, ALU.mult),
            ("r2acc", "c1a", "c1b"), ("f5tmp",))
        dma("sp", outD[k * 128:(k + 1) * 128, :], f5tmp[:], ("f5tmp",), (f"outD{k}",))
    bar()
    return nc, P


def _build_and_emit(debug=False):
    nc, P = build_nc(debug=debug)
    import contextlib
    es = contextlib.ExitStack()
    sems = {e: es.enter_context(nc.semaphore(f"sem_{e}")) for e in ENGINES}
    block = es.enter_context(nc.Block())
    with es:
        block.tensor(lambda eng: P.emit("pe", eng, sems))
        block.vector(lambda eng: P.emit("dve", eng, sems))
        block.scalar(lambda eng: P.emit("act", eng, sems))
        block.gpsimd(lambda eng: P.emit("pool", eng, sems))
        block.sync(lambda eng: P.emit("sp", eng, sems))
    return nc



def _token_ids():
    ids = []
    for c in range(NCORE):
        rows = [np.arange((q * NCORE + c) * 128, (q * NCORE + c + 1) * 128) for q in range(NCH)]
        ids.append(np.concatenate(rows))
    return ids


def _host_prep(inputs):
    x = np.asarray(inputs["L0_attn_x"], dtype=np.float32).reshape(TOK, C)
    ids = _token_ids()
    regroup = lambda v: np.ascontiguousarray(np.asarray(v, np.float32).reshape(G, 128).T)
    regroup2 = lambda m: np.ascontiguousarray(
        np.asarray(m, np.float32).reshape(G, 128, -1).transpose(1, 0, 2).reshape(128, -1))
    ident = np.eye(128, dtype=np.float32)
    lstrict = np.triu(np.ones((128, 128), np.float32), 1)
    gg = np.arange(GCH)
    lmask = ((gg[:, None] < gg[None, :]) & (gg[:, None] // 32 == gg[None, :] // 32)).astype(np.float32)
    gofr = (gg % 16) * 8 + gg // 16
    lstrg = (gofr[:, None] < gg[None, :]).astype(np.float32)
    iotae_h = np.broadcast_to(np.arange(E, dtype=np.float32), (128, E)).copy()
    common = {
        "wqT": np.ascontiguousarray(np.asarray(inputs["L0_attn_Wq"], np.float32).T),
        "wkT": np.ascontiguousarray(np.asarray(inputs["L0_attn_Wk"], np.float32).T),
        "wvT": np.ascontiguousarray(np.asarray(inputs["L0_attn_Wv"], np.float32).T),
        "wgT": np.ascontiguousarray(np.asarray(inputs["L0_attn_Wg"], np.float32).T),
        "woT": np.ascontiguousarray(np.asarray(inputs["L0_attn_Wo"], np.float32).T),
        "w0D": np.ascontiguousarray(np.asarray(inputs["L0_attn_W0"], np.float32)),
        "aD": regroup2(inputs["L0_attn_A_init"]),
        "bD": np.ascontiguousarray(np.asarray(inputs["L0_attn_B_init"], np.float32)),
        "lngD": regroup(inputs["L0_attn_ln_g"]),
        "lnbD": regroup(inputs["L0_attn_ln_b"]),
        "lsgD": regroup(inputs["L0_attn_ls_g"]),
        "llrD": regroup(inputs["L0_attn_log_lr"]),
        "rwtD": regroup2(np.asarray(inputs["L0_ffn_router_w"], np.float32).T),
        "rgD": np.ascontiguousarray(np.asarray(inputs["L0_ffn_rg"], np.float32)),
        "ruD": np.ascontiguousarray(np.asarray(inputs["L0_ffn_ru"], np.float32)),
        "rdD": np.ascontiguousarray(np.asarray(inputs["L0_ffn_rd"], np.float32)),
        "sgD": np.ascontiguousarray(np.asarray(inputs["L0_ffn_sg"], np.float32)),
        "suD": np.ascontiguousarray(np.asarray(inputs["L0_ffn_su"], np.float32)),
        "sdD": np.ascontiguousarray(np.asarray(inputs["L0_ffn_sd"], np.float32)),
        "identD": ident,
        "lstrD": lstrict,
        "iotaeD": iotae_h,
        "e448D": iotae_h * float(LCAP),
        "trashD": (TRASH + np.arange(128, dtype=np.float32)).reshape(128, 1),
        "lstrgD": lstrg,
    }
    in_maps = []
    for c in range(NCORE):
        own_g = np.arange(NCH) * NCORE + c
        q1hot = np.zeros((128, NCH), np.float32)
        for q in range(NCH):
            q1hot[q * NCORE + c, q] = 1.0
        tok = ids[c]
        pos = (tok % T_).astype(np.float64) + 1.0
        ps = (1.0 / (1.0 + 0.1 * np.log(pos))).astype(np.float32)
        m = dict(common)
        m["xT"] = np.ascontiguousarray(x[tok].T)
        m["lmoD"] = np.ascontiguousarray(lmask[gofr][:, own_g])
        m["q1hD"] = q1hot
        m["psD"] = np.broadcast_to(ps, (128, NT)).copy()
        in_maps.append(m)
    return in_maps, ids


_NC_CACHE = {}


def kernel(**inputs):
    in_maps, ids = _host_prep(inputs)
    if "nc" not in _NC_CACHE:
        _NC_CACHE["nc"] = _build_and_emit(debug=False)
    nc = _NC_CACHE["nc"]
    res = run_bass_kernel_spmd(nc, in_maps, core_ids=list(range(NCORE)))
    out = np.empty((TOK, C), dtype=np.float32)
    for c in range(NCORE):
        out[ids[c]] = res.results[c]["outD"]
    return out.reshape(B_, T_, C)
